# version 12
# baseline (speedup 1.0000x reference)
import contextlib
import numpy as np
import concourse.bass as bass
import concourse.mybir as mybir
from concourse.bass_utils import run_bass_kernel_spmd

F32 = mybir.dt.float32
BF16 = mybir.dt.bfloat16
I32 = mybir.dt.int32
AF = mybir.ActivationFunctionType
ALU = mybir.AluOpType

D = 2048
T = 2048
TT = 16
DC = 16
NH = 16
HD = 128
SEQ = 8192
NCHUNK = 8
RSD = 1.0 / np.sqrt(128.0)

ENGINES = ("tensor", "vector", "scalar", "gpsimd", "sync")


class _Dummy:
    def __call__(self, *a, **k):
        return self
    def __getattr__(self, name):
        return self


_DUMMY = _Dummy()


class Sem:
    def __init__(self, handle, name):
        self.h = handle
        self.name = name
        self.n = 0


class Em:
    def __init__(self, nc, active, engine_obj, sems):
        self.nc = nc
        self.active = active
        self._eng = engine_obj
        self.sems = sems

    def __getattr__(self, name):
        if name in ENGINES:
            return self._eng if name == self.active else _DUMMY
        raise AttributeError(name)

    def eng(self, name):
        return self._eng if name == self.active else _DUMMY

    def inc(self, instr, sem, k=1):
        sem.n += k
        instr.then_inc(sem.h, k)
        return sem.n

    def wait(self, engine_name, sem, val=None):
        self.eng(engine_name).wait_ge(sem.h, sem.n if val is None else val)

    def chain(self, engine_name, instr, sem):
        sem.n += 1
        instr.then_inc(sem.h, 1)
        self.eng(engine_name).wait_ge(sem.h, sem.n)


class Prog:
    def __init__(self, name="prog"):
        self.nc = bass.Bass("TRN2", target_bir_lowering=False)
        self.ctx = contextlib.ExitStack()
        self.name = name
        self._sb = {}
        self._ps = {}
        self._sems = {}
        self._dram = {}

    def dram(self, name, shape, dtype, kind="Internal"):
        ap = self.nc.dram_tensor(name, list(shape), dtype, kind=kind).ap()
        self._dram[name] = ap
        return ap

    def sb(self, name, shape, dtype):
        t = self.ctx.enter_context(self.nc.sbuf_tensor(name, list(shape), dtype))
        self._sb[name] = t
        return t

    def ps(self, name, shape, dtype=F32):
        t = self.ctx.enter_context(self.nc.psum_tensor(name, list(shape), dtype))
        self._ps[name] = t
        return t

    def sem(self, name):
        h = self.ctx.enter_context(self.nc.semaphore(name))
        s = Sem(h, name)
        self._sems[name] = s
        return s

    def emit(self, body):
        block = self.ctx.enter_context(self.nc.Block())
        saved = {n: s.n for n, s in self._sems.items()}

        def run_pass(engine_name, engine_obj):
            for n, s in self._sems.items():
                s.n = saved[n]
            body(Em(self.nc, engine_name, engine_obj, self._sems))

        @block.tensor
        def _(t):
            run_pass("tensor", t)

        @block.vector
        def _(v):
            run_pass("vector", v)

        @block.scalar
        def _(s):
            run_pass("scalar", s)

        @block.gpsimd
        def _(g):
            run_pass("gpsimd", g)

        @block.sync
        def _(sy):
            run_pass("sync", sy)

        self.ctx.close()
        return self.nc




EPS1 = 1e-5
EPS2 = 1e-6
CLIPF = (128.0 - 1e-6) / 128.0


def alloc_common(p):
    R = {}
    R["x"] = p.sb("x_sb", [128, TT * D], BF16)
    R["a"] = p.sb("a_sb", [128, 32768], BF16)
    R["w"] = p.sb("w_sb", [128, 16384], BF16)
    R["c"] = p.sb("c_sb", [128, 8192], BF16)
    R["f"] = p.sb("f_sb", [128, 4096], F32)
    R["cols"] = p.sb("cols_sb", [128, 512], F32)
    R["r1"] = p.sb("r1_sb", [1, 1024], F32)
    R["r2"] = p.sb("r2_sb", [1, 2048], BF16)
    R["ones1"] = p.sb("ones1_sb", [1, 128], BF16)
    R["onesc"] = p.sb("onesc_sb", [128, 4], BF16)
    R["ones1f"] = p.sb("ones1f_sb", [1, 128], F32)
    R["onescf"] = p.sb("onescf_sb", [128, 4], F32)
    R["zero"] = p.sb("zero_sb", [128, 512], BF16)
    R["eps"] = p.sb("eps_sb", [128, 2], F32)
    R["ioff"] = p.sb("ioff_sb", [1, 2], I32)
    for i in range(8):
        R[f"p{i}"] = p.ps(f"p{i}", [128, 512], F32)
    return R


def std_sems(p):
    return {
        "init": p.sem("init"),
        "ld": p.sem("ld"),
        "ev": p.sem("ev"),
        "st": p.sem("st"),
        "tr": p.sem("tr"),
        "mm": p.sem("mm"),
        "wg": p.sem("wg"),
        "cc": p.sem("cc"),
        "out": p.sem("out"),
        "bar": p.sem("bar"),
        "ld2": p.sem("ld2"),
        "st2": p.sem("st2"),
        "ev2": p.sem("ev2"),
    }


def emit_const_init(p, em, R, s):
    g = em.gpsimd
    g.memset(R["ones1"][:], 1.0)
    g.memset(R["onesc"][:], 1.0)
    g.memset(R["ones1f"][:], 1.0)
    g.memset(R["onescf"][:], 1.0)
    g.memset(R["eps"][:, 0:1], EPS1)
    g.memset(R["eps"][:, 1:2], EPS2)
    em.inc(g.memset(R["zero"][:], 0.0), s["init"], 1)
    for e in ENGINES:
        em.wait(e, s["init"])


def barrier(em, bar):
    for e in ENGINES:
        em.eng(e).nop().then_inc(bar.h, 1)
    bar.n += 5
    for e in ENGINES:
        em.eng(e).wait_ge(bar.h, bar.n)


def emit_load_x(p, em, R, x_hbm, s):
    g = em.gpsimd
    x = R["x"]
    for t in range(TT):
        em.inc(g.dma_start(out=x[:, t * D:(t + 1) * D],
                           in_=x_hbm[t * 128:(t + 1) * 128, :]), s["ld"], 16)
    for e in ("vector", "scalar", "tensor", "sync"):
        em.wait(e, s["ld"])


def emit_store_x(p, em, R, xo_hbm, s):
    g = em.gpsimd
    x = R["x"]
    for t in range(TT):
        em.inc(g.dma_start(out=xo_hbm[t * 128:(t + 1) * 128, :],
                           in_=x[:, t * D:(t + 1) * D]), s["out"], 16)
    em.wait("gpsimd", s["out"])


def emit_ln_stats(p, em, R, s, mode="ln", src=None, ncols=TT, width=D,
                  eps1=EPS1):
    cols, f = R["cols"], R["f"]
    x = R["x"]
    if src is None:
        src = lambda t: x[:, t * D:(t + 1) * D]
    v, sc = em.vector, em.scalar
    n = 1.0 / width
    S = cols[:, 0:ncols]
    Q = cols[:, 16:16 + ncols]
    M = cols[:, 32:32 + ncols]
    N = cols[:, 48:48 + ncols]
    for t in range(ncols):
        xt = src(t)
        i1 = v.tensor_reduce(S[:, t:t + 1], xt, mybir.AxisListType.X, ALU.add)
        i2 = sc.activation(f[:, 0:2048], xt, AF.Square, accum_out=Q[:, t:t + 1])
        if mode == "bitlin":
            v.tensor_reduce(M[:, t:t + 1], xt, mybir.AxisListType.X, ALU.max)
            i1 = v.tensor_reduce(N[:, t:t + 1], xt, mybir.AxisListType.X, ALU.min)
        if t == ncols - 1:
            em.inc(i1, s["ev"], 1)
            em.inc(i2, s["ev"], 1)
    em.wait("vector", s["ev"])
    nm = cols[:, 64:64 + ncols]
    nm2 = cols[:, 160:160 + ncols]
    var = cols[:, 176:176 + ncols]
    srt = cols[:, 192:192 + ncols]
    rstd = cols[:, 80:80 + ncols]
    em.chain("vector", v.tensor_scalar(nm, S, -n, None, ALU.mult), s["ev"])
    em.chain("vector", v.tensor_tensor(nm2, nm, nm, ALU.mult), s["ev"])
    em.inc(v.scalar_tensor_tensor(var, Q, n, nm2, ALU.mult, ALU.subtract),
           s["ev"], 1)
    em.wait("scalar", s["ev"])
    epsc = R["eps"][:, 0:1] if eps1 == EPS1 else R["eps"][:, 1:2]
    em.inc(sc.activation(srt, var, AF.Sqrt, bias=epsc), s["ev"], 1)
    em.wait("vector", s["ev"])
    em.chain("vector", v.reciprocal(rstd, srt), s["ev"])
    if mode == "bitlin":
        A = cols[:, 96:96 + ncols]
        boff = cols[:, 112:112 + ncols]
        gam = cols[:, 128:128 + ncols]
        gneg = cols[:, 144:144 + ncols]
        v2 = cols[:, 208:208 + ncols]
        r2s = cols[:, 224:224 + ncols]
        r2 = cols[:, 240:240 + ncols]
        t1 = cols[:, 256:256 + ncols]
        t2 = cols[:, 272:272 + ncols]
        em.chain("vector", v.tensor_tensor(v2, var, rstd, ALU.mult), s["ev"])
        em.inc(v.tensor_tensor(v2, v2, rstd, ALU.mult), s["ev"], 1)
        em.wait("scalar", s["ev"])
        em.inc(sc.activation(r2s, v2, AF.Sqrt, bias=R["eps"][:, 1:2]), s["ev"], 1)
        em.wait("vector", s["ev"])
        em.chain("vector", v.reciprocal(r2, r2s), s["ev"])
        em.chain("vector", v.tensor_tensor(A, rstd, r2, ALU.mult), s["ev"])
        em.chain("vector", v.tensor_tensor(t1, M, nm, ALU.add), s["ev"])
        em.chain("vector", v.tensor_tensor(t2, N, nm, ALU.add), s["ev"])
        em.chain("vector", v.tensor_scalar(t2, t2, -1.0, None, ALU.mult), s["ev"])
        em.chain("vector", v.tensor_tensor(gam, t1, t2, ALU.max), s["ev"])
        em.chain("vector", v.tensor_tensor(gam, gam, A, ALU.mult), s["ev"])
        em.chain("vector", v.tensor_scalar(gam, gam, CLIPF, None, ALU.mult),
                 s["ev"])
        em.chain("vector", v.tensor_scalar(gneg, gam, -1.0, None, ALU.mult),
                 s["ev"])
        em.chain("vector", v.tensor_tensor(boff, nm, A, ALU.mult), s["ev"])


def emit_ln_to_aT(p, em, R, xp_hbm, s, mode="ln", to_aT=True):
    x, a, c, cols, f = R["x"], R["a"], R["c"], R["cols"], R["f"]
    v, sy = em.vector, em.sync
    emit_ln_stats(p, em, R, s, mode=mode)
    nm = cols[:, 64:80]
    rstd = cols[:, 80:96]
    A = cols[:, 96:112]
    boff = cols[:, 112:128]
    gam = cols[:, 128:144]
    gneg = cols[:, 144:160]
    store_tgt = []
    for t in range(TT):
        xt = x[:, t * D:(t + 1) * D]
        stage = c[:, (t % 2) * D:(t % 2) * D + D]
        if t >= 2:
            em.wait("vector", s["st"], store_tgt[t - 2])
        if mode == "ln":
            i = v.tensor_scalar(stage, xt, nm[:, t:t + 1], rstd[:, t:t + 1],
                                ALU.add, ALU.mult)
        else:
            tmp = f[:, 2048:4096]
            em.chain("vector",
                     v.tensor_scalar(tmp, xt, A[:, t:t + 1], boff[:, t:t + 1],
                                     ALU.mult, ALU.add), s["ev"])
            i = v.tensor_scalar(stage, tmp, gam[:, t:t + 1], gneg[:, t:t + 1],
                                ALU.min, ALU.max)
        em.inc(i, s["ev"], 1)
        em.wait("sync", s["ev"])
        tgt = em.inc(sy.dma_start(out=xp_hbm[t * 128:(t + 1) * 128, :], in_=stage),
                     s["st"], 16)
        store_tgt.append(tgt)
    em.wait("sync", s["st"])
    if to_aT:
        for dch in range(DC):
            em.inc(sy.dma_start(out=a[:, dch * T:(dch + 1) * T],
                                in_=xp_hbm[:, dch * 128:(dch + 1) * 128],
                                transpose=True), s["tr"], 16)
        em.wait("tensor", s["tr"])


def emit_dequant(p, em, R, pk_hbm, scT_hbm, out_hbm, n_o, n_w, s,
                 eng="vector", arena=None, q="sync", sems=("ld", "st", "ev")):
    c = R[arena] if arena else R["c"]
    Gtot = n_w * 8 // 128
    ntiles = n_o // 128
    ve = em.eng(eng)
    sy = em.eng(q)
    s_ld, s_st, s_ev = s[sems[0]], s[sems[1]], s[sems[2]]
    WS = 256
    slices = []
    w0 = 0
    while w0 < n_w:
        slices.append((w0, min(WS, n_w - w0)))
        w0 += WS
    pkc = 2 * n_w
    wsb = c[:, 6144:] if arena else R["w"]
    assert (not arena) or (6144 + 2 * pkc + 2 * Gtot <= 32768)
    wa = c[:, 4096:4608].bitcast(I32)
    wb = c[:, 4608:5120].bitcast(I32)
    wc = c[:, 5120:5632].bitcast(I32)
    wd = (c[:, 5632:6144] if arena else R["f"][:, 2048:2304]).bitcast(I32)
    pk = [wsb[:, 0:pkc].bitcast(I32),
          wsb[:, pkc:2 * pkc].bitcast(I32)]
    scb = 2 * pkc
    scf = [wsb[:, scb + i * Gtot:scb + (i + 1) * Gtot]
           .bitcast(mybir.dt.float16) for i in range(2)]
    sc32 = (c[:, 30000:30000 + 2 * Gtot].bitcast(F32) if arena
            else R["f"][:, 0:Gtot])
    ld_tgt = []
    comp_tgt = []
    buf_store = {}
    outbuf_ct = 0
    for i in range(ntiles + 1):
        if i < ntiles:
            if i >= 2:
                em.wait("sync", s_ev, comp_tgt[i - 2])
            em.inc(sy.dma_start(out=pk[i % 2][:],
                                in_=pk_hbm[i * 128:(i + 1) * 128, :]),
                   s_ld, 16)
            t1 = em.inc(sy.dma_start(out=scf[i % 2][:],
                                     in_=scT_hbm[i * 128:(i + 1) * 128, :]),
                        s_ld, 16)
            ld_tgt.append(t1)
        if i == 0:
            continue
        j = i - 1
        b = j % 2
        em.wait(eng, s_ld, ld_tgt[j])
        em.chain(eng, ve.tensor_copy(sc32, scf[b][:]), s_ev)
        for (w0, ws) in slices:
            ob = outbuf_ct % 2
            outbuf_ct += 1
            out_bf = c[:, ob * 2048:ob * 2048 + 8 * ws]
            out_i = out_bf.bitcast(I32)
            if ob in buf_store:
                em.wait(eng, s_st, buf_store[ob])
            pks = pk[b][:, w0:w0 + ws]
            for pp in range(4):
                em.chain(eng, ve.tensor_scalar(wa[:, 0:ws], pks, 8 * pp, 0xFF,
                                               ALU.logical_shift_right,
                                               ALU.bitwise_and), s_ev)
                em.chain(eng, ve.tensor_scalar(wa[:, 0:ws], wa[:, 0:ws], 4097,
                                               None, ALU.mult), s_ev)
                em.chain(eng, ve.tensor_scalar(wb[:, 0:ws], wa[:, 0:ws],
                                               0x80008, 12, ALU.bitwise_and,
                                               ALU.logical_shift_left), s_ev)
                em.chain(eng, ve.tensor_scalar(wc[:, 0:ws], wa[:, 0:ws],
                                               7, None, ALU.bitwise_and),
                         s_ev)
                em.chain(eng, ve.tensor_scalar(wc[:, 0:ws], wc[:, 0:ws],
                                               252, None, ALU.add), s_ev)
                em.chain(eng, ve.tensor_scalar(wc[:, 0:ws], wc[:, 0:ws],
                                               6, None,
                                               ALU.logical_shift_left), s_ev)
                em.chain(eng, ve.tensor_scalar(wa[:, 0:ws], wa[:, 0:ws],
                                               0x70000, None, ALU.bitwise_and),
                         s_ev)
                em.chain(eng, ve.tensor_scalar(wa[:, 0:ws], wa[:, 0:ws],
                                               0xFC0000, None, ALU.add),
                         s_ev)
                em.chain(eng, ve.tensor_scalar(wa[:, 0:ws], wa[:, 0:ws],
                                               6, None,
                                               ALU.logical_shift_left), s_ev)
                em.chain(eng, ve.tensor_tensor(wa[:, 0:ws], wa[:, 0:ws],
                                               wc[:, 0:ws], ALU.bitwise_or),
                         s_ev)
                ubf = wa[:, 0:ws].bitcast(BF16)
                tbf = wd[:, 0:ws].bitcast(BF16)
                em.chain(eng, ve.tensor_scalar(tbf, ubf, 2.0, 1.0, ALU.mult,
                                               ALU.subtract), s_ev)
                em.chain(eng, ve.tensor_tensor(ubf, ubf, tbf, ALU.min), s_ev)
                dst = out_i.rearrange("q (w four) -> q w four", four=4)[:, :, pp]
                em.chain(eng, ve.tensor_tensor(dst, wa[:, 0:ws], wb[:, 0:ws],
                                               ALU.bitwise_or), s_ev)
            g0 = w0 // 16
            ng = ws // 16
            last = None
            for g in range(ng):
                last = ve.tensor_scalar(out_bf[:, g * 128:(g + 1) * 128],
                                        out_bf[:, g * 128:(g + 1) * 128],
                                        sc32[:, g0 + g:g0 + g + 1], None,
                                        ALU.mult)
            em.chain(eng, last, s_ev)
            em.wait("sync", s_ev)
            t2 = em.inc(sy.dma_start(
                out=out_hbm[j * 128:(j + 1) * 128, w0 * 8:w0 * 8 + ws * 8],
                in_=out_bf), s_st, 16)
            buf_store[ob] = t2
        comp_tgt.append(s_ev.n)
    em.wait("sync", s_st)
    em.wait(eng, s_st)


def emit_ffn0(p, em, R, wgu_hbm, wdn_hbm, h_hbm, y_hbm, s):
    a, w, c, cols, f = R["a"], R["w"], R["c"], R["cols"], R["f"]
    t, v, sc, sy = em.tensor, em.vector, em.scalar, em.sync
    I = 5632
    NOC = I // 512
    mm_prev_ocp = 0
    ev_bank = {}
    st_stage = {}
    for ocp in range(NOC):
        em.wait("sync", s["mm"], mm_prev_ocp)
        for half, row0 in ((0, ocp * 512), (1, I + ocp * 512)):
            for dch in range(DC):
                ins = sy.dma_start(
                    out=w[:, half * 8192 + dch * 512:half * 8192 + dch * 512 + 512],
                    in_=wgu_hbm[row0:row0 + 512, dch * 128:(dch + 1) * 128],
                    transpose=True)
                s["wg"].n += 16
                ins.then_inc(s["wg"].h, 16)
        em.wait("tensor", s["wg"])
        for tt in range(TT):
            bg, bu = (tt % 2) * 2, (tt % 2) * 2 + 1
            pg, pu = R[f"p{bg}"], R[f"p{bu}"]
            for b in (bg, bu):
                if b in ev_bank:
                    em.wait("tensor", s["ev"], ev_bank[b])
            for dch in range(DC):
                lhs = a[:, dch * T + tt * 128:dch * T + tt * 128 + 128]
                t.matmul(pg[:], lhs, w[:, dch * 512:dch * 512 + 512],
                         start=(dch == 0), stop=(dch == DC - 1))
                ins = t.matmul(pu[:], lhs, w[:, 8192 + dch * 512:8192 + dch * 512 + 512],
                               start=(dch == 0), stop=(dch == DC - 1))
            mmt = em.inc(ins, s["mm"], 1)
            em.wait("scalar", s["mm"], mmt)
            sg = f[:, 2048 + (tt % 2) * 512:2048 + (tt % 2) * 512 + 512]
            em.inc(sc.activation(sg, pg[:], AF.Silu), s["ev"], 1)
            em.wait("vector", s["ev"])
            stg = (tt % 4)
            stage = c[:, stg * 512:stg * 512 + 512]
            if stg in st_stage:
                em.wait("vector", s["st"], st_stage[stg])
            ins = v.tensor_tensor(stage, sg, pu[:], ALU.mult)
            evt = em.inc(ins, s["ev"], 1)
            ev_bank[bg] = evt
            ev_bank[bu] = evt
            em.wait("sync", s["ev"], evt)
            st_stage[stg] = em.inc(
                sy.dma_start(out=h_hbm[tt * 128:(tt + 1) * 128,
                                       ocp * 512:ocp * 512 + 512], in_=stage),
                s["st"], 16)
        mm_prev_ocp = s["mm"].n
    em.wait("sync", s["st"])
    NICH = I // 128
    ev_bank = {}
    for dhalf in range(2):
        em.wait("sync", s["mm"])
        for ich in range(NICH):
            dst = (a[:, ich * 1024:(ich + 1) * 1024] if ich < 32 else
                   w[:, (ich - 32) * 1024:(ich - 31) * 1024])
            ins = sy.dma_start(
                out=dst,
                in_=wdn_hbm[dhalf * 1024:(dhalf + 1) * 1024,
                            ich * 128:(ich + 1) * 128],
                transpose=True)
            s["wg"].n += 16
            ins.then_inc(s["wg"].h, 16)
        em.wait("tensor", s["wg"])
        for tp in range(8):
            ht = [c[:, 4096 + k * 256:4096 + (k + 1) * 256] for k in range(2)]
            mm_base = s["mm"].n
            for b in range(4):
                if b in ev_bank:
                    em.wait("tensor", s["ev"], ev_bank[b])
            for ich in range(NICH):
                hb = ich % 2
                if ich >= 2:
                    em.wait("sync", s["mm"], mm_base + ich - 1)
                else:
                    em.wait("sync", s["mm"], mm_base)
                ins = sy.dma_start(
                    out=ht[hb],
                    in_=h_hbm[tp * 256:(tp + 1) * 256,
                              ich * 128:(ich + 1) * 128], transpose=True)
                tld = em.inc(ins, s["tr"], 16)
                em.wait("tensor", s["tr"], tld)
                wt_t = (a[:, ich * 1024:(ich + 1) * 1024] if ich < 32 else
                        w[:, (ich - 32) * 1024:(ich - 31) * 1024])
                for ttl in range(2):
                    for dcol in range(2):
                        ins = t.matmul(
                            R[f"p{ttl * 2 + dcol}"][:],
                            ht[hb][:, ttl * 128:(ttl + 1) * 128],
                            wt_t[:, dcol * 512:(dcol + 1) * 512],
                            start=(ich == 0), stop=(ich == NICH - 1))
                ins.then_inc(s["mm"].h, 1)
                s["mm"].n += 1
            mmt = s["mm"].n
            em.wait("scalar", s["mm"], mmt)
            evt = None
            for ttl in range(2):
                tt = tp * 2 + ttl
                for dcol in range(2):
                    b = ttl * 2 + dcol
                    ps = R[f"p{b}"]
                    stage = c[:, b * 512:(b + 1) * 512]
                    em.wait("scalar", s["st"], st_stage.get(("dn", b), 0))
                    em.inc(sc.copy(stage, ps[:]), s["ev"], 1)
                    qcol = cols[:, 256 + tt * 4 + dhalf * 2 + dcol:
                                257 + tt * 4 + dhalf * 2 + dcol]
                    evt = em.inc(sc.activation(f[:, 2048:2560], ps[:], AF.Square,
                                               accum_out=qcol), s["ev"], 1)
                    em.wait("sync", s["ev"], evt)
                    st_stage[("dn", b)] = em.inc(
                        sy.dma_start(out=y_hbm[tt * 128:(tt + 1) * 128,
                                               dhalf * 1024 + dcol * 512:
                                               dhalf * 1024 + dcol * 512 + 512],
                                     in_=stage), s["st"], 16)
                    ev_bank[b] = evt
        dn_prev = s["mm"].n
    em.wait("sync", s["st"])


def emit_rms_res(p, em, R, y_hbm, s, scale=0.5):
    x, c, cols, f = R["x"], R["c"], R["cols"], R["f"]
    v, sc, g = em.vector, em.scalar, em.gpsimd
    ss = cols[:, 448:448 + TT]
    for tt in range(TT):
        src = cols[:, 256 + tt * 4:256 + tt * 4 + 4]
        i = v.tensor_reduce(ss[:, tt:tt + 1], src, mybir.AxisListType.X, ALU.add)
    em.chain("vector", i, s["ev"])
    srt = cols[:, 464:464 + TT]
    rr = cols[:, 480:480 + TT]
    em.inc(v.tensor_scalar(ss, ss, 1.0 / D, None, ALU.mult), s["ev"], 1)
    em.wait("scalar", s["ev"])
    em.inc(sc.activation(srt, ss, AF.Sqrt, bias=R["eps"][:, 1:2]), s["ev"], 1)
    em.wait("vector", s["ev"])
    em.chain("vector", v.reciprocal(rr, srt), s["ev"])
    em.chain("vector", v.tensor_scalar(rr, rr, scale, None, ALU.mult), s["ev"])
    for tt in range(TT):
        yb = tt % 2
        ystage = c[:, yb * 2048:(yb + 1) * 2048]
        ins = g.dma_start(out=ystage, in_=y_hbm[tt * 128:(tt + 1) * 128, :])
        tld = em.inc(ins, s["ld"], 16)
        em.wait("vector", s["ld"], tld)
        xt = x[:, tt * D:(tt + 1) * D]
        ins = v.scalar_tensor_tensor(xt, ystage, rr[:, tt:tt + 1], xt,
                                     ALU.mult, ALU.add)
        evt = em.inc(ins, s["ev"], 1)
        if tt + 2 < TT:
            em.wait("gpsimd", s["ev"], evt)
    em.wait("vector", s["ev"])



def emit_qkv_h(p, em, R, wq_hbm, xn_full, q_hbm, k_hbm, v_hbm, qn_hbm, kn_hbm,
               s, bias_hbm=None):
    a, w, c, cols, f = R["a"], R["w"], R["c"], R["cols"], R["f"]
    t, v, sc, sy, g = em.tensor, em.vector, em.scalar, em.sync, em.gpsimd
    em.wait("gpsimd", s["mm"])
    em.wait("gpsimd", s["ev"])
    for oc in range(8):
        ins = g.dma_start(out=a[:, oc * 2048:(oc + 1) * 2048]
                          .rearrange("q (dch o) -> q dch o", o=128),
                          in_=wq_hbm[:, oc * 128:(oc + 1) * 128]
                          .rearrange("(dch q) o -> q dch o", q=128))
        em.inc(ins, s["wg"], 16)
    ins = g.dma_start(out=a[:, 16384:24576]
                      .rearrange("q (dch o) -> q dch o", o=512),
                      in_=wq_hbm[:, 1024:1536]
                      .rearrange("(dch q) o -> q dch o", q=128))
    em.inc(ins, s["wg"], 16)
    if bias_hbm is not None:
        for oc in range(4):
            ins = g.dma_start(out=f[:, 3776 + oc:3777 + oc],
                              in_=bias_hbm[oc * 128:(oc + 1) * 128]
                              .rearrange("(x y) -> x y", y=1))
            em.inc(ins, s["wg"], 16)
        for oc in range(4):
            ins = g.dma_start(out=f[:, 3780 + oc:3781 + oc],
                              in_=bias_hbm[512 + oc * 128:512 + (oc + 1) * 128]
                              .rearrange("(x y) -> x y", y=1))
            em.inc(ins, s["wg"], 16)
        ins = g.dma_start(out=R["r1"][0:1, 512:1024],
                          in_=bias_hbm[1024:1536].rearrange("(x y) -> x y", x=1))
        em.inc(ins, s["wg"], 16)
        em.wait("vector", s["wg"])
        em.chain("vector", v.tensor_copy(f[0:1, 2560:3072],
                                         R["r1"][0:1, 512:1024]), s["ev"])
        em.wait("tensor", s["ev"])
        ins = t.matmul(R["p3"][:], R["ones1f"][:], f[0:1, 2560:3072],
                       start=True, stop=True)
        bmt = em.inc(ins, s["mm"], 1)
        em.wait("vector", s["mm"], bmt)
        em.chain("vector", v.tensor_copy(f[:, 2560:3072], R["p3"][:]), s["ev"])
    em.wait("sync", s["wg"])
    ev_bank = {}
    tr_prev = []
    for tc in range(16):
        em.wait("sync", s["mm"])
        for dch in range(DC):
            ins = sy.dma_start(out=w[:, dch * 512:(dch + 1) * 512],
                               in_=xn_full[tc * 512:(tc + 1) * 512,
                                           dch * 128:(dch + 1) * 128],
                               transpose=True)
            em.inc(ins, s["tr"], 16)
        em.wait("tensor", s["tr"])
        for oc in range(8):
            b = oc % 2
            pq = R[f"p{b}"]
            if b in ev_bank:
                em.wait("tensor", s["ev"], ev_bank[b])
            if "pn" in ev_bank:
                em.wait("tensor", s["ev"], ev_bank["pn"])
            for dch in range(DC):
                ins = t.matmul(pq[:], a[:, oc * 2048 + dch * 128:
                                        oc * 2048 + (dch + 1) * 128],
                               w[:, dch * 512:(dch + 1) * 512],
                               start=(dch == 0), stop=(dch == DC - 1))
            mmt = em.inc(ins, s["mm"], 1)
            em.wait("vector", s["mm"], mmt)
            stage = c[:, b * 512:b * 512 + 512]
            if ("st", b) in ev_bank:
                em.wait("vector", s["st"], ev_bank[("st", b)])
            isq = oc < 4
            if bias_hbm is not None:
                bcol = f[:, 3776 + oc:3777 + oc]
                if isq:
                    i2 = v.tensor_scalar(stage, pq[:], bcol, RSD,
                                         ALU.add, ALU.mult)
                else:
                    i2 = v.tensor_scalar(stage, pq[:], bcol, None, ALU.add)
            else:
                if isq:
                    i2 = v.tensor_scalar(stage, pq[:], RSD, None, ALU.mult)
                else:
                    i2 = v.tensor_copy(stage, pq[:])
            em.chain("vector", i2, s["ev"])
            ev_bank[b] = s["ev"].n
            em.wait("scalar", s["ev"], ev_bank[b])
            sq = c[:, 2048 + b * 512:2048 + b * 512 + 512]
            i3 = sc.activation(sq, stage, AF.Square)
            sqt = em.inc(i3, s["ev"], 1)
            em.wait("tensor", s["ev"], sqt)
            pn = R["p2"]
            ins = t.matmul(pn[0:1, :], R["onesc"][:, 0:1], sq,
                           start=True, stop=True)
            nmt = em.inc(ins, s["mm"], 1)
            em.wait("scalar", s["mm"], nmt)
            nrow = f[0:1, 2048:2560]
            if isq:
                i4 = sc.activation(nrow, pn[0:1, :], AF.Sqrt)
            else:
                i4 = sc.copy(nrow, pn[0:1, :])
            qnt = em.inc(i4, s["ev"], 1)
            ev_bank["pn"] = qnt
            em.wait("sync", s["ev"], qnt)
            dstn = qn_hbm if isq else kn_hbm
            em.inc(sy.dma_start(out=dstn[oc % 4:oc % 4 + 1,
                                         tc * 512:(tc + 1) * 512],
                                in_=nrow), s["st"], 16)
            dst = q_hbm if isq else k_hbm
            ev_bank[("st", b)] = em.inc(
                sy.dma_start(out=dst[(oc % 4) * 128:(oc % 4) * 128 + 128,
                                     tc * 512:(tc + 1) * 512],
                             in_=stage), s["st"], 16)
        for tt4 in range(4):
            b = tt4 % 2
            pq = R[f"p{4 + b}"]
            if ("v", b) in ev_bank:
                em.wait("tensor", s["ev"], ev_bank[("v", b)])
            for dch in range(DC):
                ins = t.matmul(pq[:],
                               w[:, dch * 512 + tt4 * 128:dch * 512 + tt4 * 128 + 128],
                               a[:, 16384 + dch * 512:16384 + (dch + 1) * 512],
                               start=(dch == 0), stop=(dch == DC - 1))
            mmt = em.inc(ins, s["mm"], 1)
            em.wait("vector", s["mm"], mmt)
            stage = c[:, 4096 + b * 512:4096 + b * 512 + 512]
            if ("vst", b) in ev_bank:
                em.wait("vector", s["st"], ev_bank[("vst", b)])
            if bias_hbm is not None:
                i2 = v.tensor_tensor(stage, pq[:], f[:, 2560:3072], ALU.add)
            else:
                i2 = v.tensor_copy(stage, pq[:])
            em.chain("vector", i2, s["ev"])
            ev_bank[("v", b)] = s["ev"].n
            em.wait("sync", s["ev"], ev_bank[("v", b)])
            ev_bank[("vst", b)] = em.inc(
                sy.dma_start(out=v_hbm[tc * 512 + tt4 * 128:
                                       tc * 512 + (tt4 + 1) * 128, :],
                             in_=stage), s["st"], 16)
    em.wait("sync", s["st"])


def emit_attn_masks(p, em, R, s):
    g = em.gpsimd
    w = R["w"]
    em.wait("gpsimd", s["mm"])
    io16 = w[:, 10240:10752].bitcast(mybir.dt.int16)
    g.iota(io16, pattern=[[1, 512]], base=0, channel_multiplier=-1,
           allow_small_or_imprecise_dtypes=True)
    ve = em.vector
    em.wait("vector", s["mm"])
    em.inc(g.engine_nop(), s["init"], 1)
    em.wait("vector", s["init"])
    for m in range(4):
        sl = w[:, 8192 + m * 512:8192 + (m + 1) * 512].bitcast(mybir.dt.int16)
        em.chain("vector", ve.tensor_scalar(sl, io16, 128 * m, None, ALU.is_lt),
                 s["ev"])
    g.iota(R["f"][:, 3520:3584], pattern=[[128, 64]], base=0,
           channel_multiplier=1, allow_small_or_imprecise_dtypes=True)
    ins = g.iota(R["r1"][0:1, 0:512], pattern=[[1, 512]], base=0,
                 channel_multiplier=0, allow_small_or_imprecise_dtypes=True)
    em.inc(ins, s["init"], 1)
    for e in ("tensor", "vector", "scalar", "sync"):
        em.wait(e, s["init"])


def emit_attn_flash_h(p, em, R, q_hbm, k_hbm, v_hbm, qn_hbm, kn_hbm, attn_cc,
                      s, alibi, slopes_hbm=None):
    a, w, c, cols, f = R["a"], R["w"], R["c"], R["cols"], R["f"]
    t, v, sc, sy, g = em.tensor, em.vector, em.scalar, em.sync, em.gpsimd
    r1, r2 = R["r1"], R["r2"]
    if alibi:
        ins = sy.dma_start(out=f[0:1, 3256:3260],
                           in_=slopes_hbm.rearrange("(x y) -> x y", x=1))
        em.inc(ins, s["tr"], 16)
        em.wait("tensor", s["tr"])
        ins = em.tensor.matmul(R["p6"][:, 0:4], R["ones1f"][:],
                               f[0:1, 3256:3260], start=True, stop=True)
        bmt = em.inc(ins, s["mm"], 1)
        em.wait("vector", s["mm"], bmt)
        em.chain("vector", v.tensor_copy(f[:, 3800:3804], R["p6"][:, 0:4]),
                 s["ev"])
        em.wait("tensor", s["ev"])
    if True:
        for hh in range(4):
            mx = f[0:1, 3230 + hh:3231 + hh]
            for piece in range(4):
                ins = sy.dma_start(out=f[0:1, 1024:1024 + 2048],
                                   in_=kn_hbm[hh:hh + 1,
                                              piece * 2048:(piece + 1) * 2048])
                tt_ = em.inc(ins, s["tr"], 16)
                em.wait("vector", s["tr"], tt_)
                if piece == 0:
                    em.chain("vector",
                             v.tensor_reduce(mx, f[0:1, 1024:3072],
                                             mybir.AxisListType.X, ALU.max),
                             s["ev"])
                else:
                    pm = f[0:1, 3250:3251]
                    em.chain("vector",
                             v.tensor_reduce(pm, f[0:1, 1024:3072],
                                             mybir.AxisListType.X, ALU.max),
                             s["ev"])
                    em.chain("vector", v.tensor_tensor(mx, mx, pm, ALU.max),
                             s["ev"])
                em.wait("sync", s["ev"])
        em.wait("scalar", s["ev"])
        last = None
        for hh in range(4):
            last = sc.activation(f[0:1, 3240 + hh:3241 + hh],
                                 f[0:1, 3230 + hh:3231 + hh], AF.Sqrt)
        em.inc(last, s["ev"], 1)
        em.wait("vector", s["ev"])

    mm_kbuf = {}
    ev_outbank = {}
    ev_pt = {}
    st_stage = {}
    for hh in range(4):
        for qt in range(16):
            g0 = qt * 512
            nksc = g0 // 128 + 4
            cmax = (g0 + 512 + 1023) // 1024
            ins = sy.dma_start(out=c[:, 6144:6656],
                               in_=q_hbm[hh * 128:(hh + 1) * 128,
                                         g0:g0 + 512])
            qtt = em.inc(ins, s["tr"], 16)
            ins = sy.dma_start(out=f[0:1, 1024:1536],
                               in_=qn_hbm[hh:hh + 1, g0:g0 + 512])
            em.inc(ins, s["tr"], 16)
            em.wait("vector", s["tr"])
            qnrow = f[0:1, 1024:1536]
            em.chain("vector",
                     v.tensor_scalar(qnrow, qnrow, f[0:1, 3240 + hh:3241 + hh],
                                     None, ALU.mult), s["ev"])
            if alibi:
                em.chain("vector",
                         v.scalar_tensor_tensor(qnrow, r1[0:1, 0:512],
                                                f[0:1, 3256 + hh:3257 + hh],
                                                qnrow, ALU.mult, ALU.add),
                         s["ev"])
            ins = v.tensor_scalar(r2[0:1, 0:512], qnrow, -1.0, None, ALU.mult)
            em.chain("vector", ins, s["ev"])
            if alibi:
                ins = v.tensor_scalar(f[:, 3584:3648], f[:, 3520:3584],
                                      1.0, -float(g0), ALU.mult, ALU.add)
                em.chain("vector", ins, s["ev"])
                ins = v.tensor_scalar(cols[:, 320:384], f[:, 3584:3648],
                                      f[:, 3800 + hh:3801 + hh], None, ALU.mult)
                em.chain("vector", ins, s["ev"])
            seeds_t = s["ev"].n
            em.wait("tensor", s["ev"], seeds_t)
            em.wait("scalar", s["ev"], seeds_t)
            em.wait("tensor", s["tr"], qtt)
            if 0 in ev_outbank:
                em.wait("tensor", s["ev"], ev_outbank[0])
            done = 0
            nchunks = cmax
            kbs = list(range(0, g0 + 512, 128))

            def issue_chunk_loads(cg):
                par = cg % 2
                if par in mm_kbuf:
                    em.wait("sync", s["mm"], mm_kbuf[par])
                    em.wait("gpsimd", s["mm"], mm_kbuf[par])
                klen = min(1024, g0 + 512 - cg * 1024)
                ins = sy.dma_start(out=c[:, par * 1024:par * 1024 + klen],
                                   in_=k_hbm[hh * 128:(hh + 1) * 128,
                                             cg * 1024:cg * 1024 + klen])
                ktt = em.inc(ins, s["tr"], 16)
                ins = g.dma_start(out=c[:, 2048 + par * 1024:
                                        2048 + par * 1024 + klen]
                                  .rearrange("q (sk d) -> q sk d", d=128),
                                  in_=v_hbm[cg * 1024:cg * 1024 + klen,
                                            hh * 128:(hh + 1) * 128]
                                  .rearrange("(sk q) d -> q sk d", q=128))
                vtt = em.inc(ins, s["ld"], 16)
                return ktt, vtt

            loadtgt = {}
            loadtgt[0] = issue_chunk_loads(0)
            if nchunks > 1:
                loadtgt[1] = issue_chunk_loads(1)

            def seed_qk(i):
                kb = kbs[i]
                cg = kb // 1024
                par = cg % 2
                ksc = (kb % 1024) // 128
                sb = 4 + i % 2
                Sp = R[f"p{sb}"]
                if ("exp", i % 2) in ev_pt:
                    em.wait("tensor", s["ev"], ev_pt[("exp", i % 2)])
                if i == 0 or kbs[i - 1] // 1024 != cg:
                    em.wait("tensor", s["tr"], loadtgt[cg][0])
                    em.wait("tensor", s["ld"], loadtgt[cg][1])
                t.matmul(Sp[:], R["ones1"][:], r2[0:1, 0:512],
                         start=True, stop=False)
                ins = t.matmul(Sp[:], c[:, par * 1024 + ksc * 128:
                                        par * 1024 + ksc * 128 + 128],
                               c[:, 6144:6656], start=False, stop=True)
                mmt = em.inc(ins, s["mm"], 1)
                em.wait("scalar", s["mm"], mmt)
                if ("pt", i % 2) in ev_pt:
                    em.wait("scalar", s["mm"], ev_pt[("pt", i % 2)])
                PT = c[:, 4096 + (i % 2) * 512:4096 + (i % 2) * 512 + 512]
                bias = (cols[:, 320 + kb // 128:321 + kb // 128]
                        if alibi else 0.0)
                ins = sc.activation(PT, Sp[:], AF.Exp, bias=bias)
                expt = em.inc(ins, s["ev"], 1)
                ev_pt[("exp", i % 2)] = expt
                gate = expt
                if kb >= g0:
                    m = (kb - g0) // 128
                    em.wait("vector", s["ev"], expt)
                    msk = w[:, 8192 + m * 512:8192 + (m + 1) * 512] \
                        .bitcast(mybir.dt.int16)
                    ins = v.copy_predicated(PT, msk, R["zero"][:])
                    gate = em.inc(ins, s["ev"], 1)
                ev_pt[("gate", i % 2)] = gate

            def sum_av(i):
                kb = kbs[i]
                cg = kb // 1024
                par = cg % 2
                ksc = (kb % 1024) // 128
                PT = c[:, 4096 + (i % 2) * 512:4096 + (i % 2) * 512 + 512]
                em.wait("tensor", s["ev"], ev_pt[("gate", i % 2)])
                first = i == 0
                last = i == nksc - 1
                t.matmul(R["p2"][0:1, :], R["onesc"][:, 0:1], PT,
                         start=first, stop=last)
                ins = t.matmul(R["p0"][:],
                               c[:, 2048 + par * 1024 + ksc * 128:
                                 2048 + par * 1024 + ksc * 128 + 128],
                               PT, start=first, stop=last)
                ev_pt[("pt", i % 2)] = em.inc(ins, s["mm"], 1)
                if kb % 1024 == 1024 - 128 or i == nksc - 1:
                    mm_kbuf[par] = s["mm"].n
                    if cg + 2 < nchunks and (cg + 2) not in loadtgt:
                        loadtgt[cg + 2] = issue_chunk_loads(cg + 2)

            seed_qk(0)
            for i in range(nksc):
                if i + 1 < nksc:
                    seed_qk(i + 1)
                sum_av(i)
            em.wait("vector", s["mm"])
            drow = f[0:1, 512:1024]
            em.chain("vector", v.tensor_copy(drow, R["p2"][0:1, :]), s["ev"])
            em.chain("vector", v.reciprocal(drow, drow), s["ev"])
            em.wait("tensor", s["ev"])
            ins = t.matmul(R["p6"][:], R["ones1f"][:], drow,
                           start=True, stop=True)
            bmt = em.inc(ins, s["mm"], 1)
            em.wait("vector", s["mm"], bmt)
            em.chain("vector", v.tensor_copy(f[:, 0:512], R["p6"][:]), s["ev"])
            stage = c[:, 7168:7680]
            if "st" in st_stage:
                em.wait("vector", s["st"], st_stage["st"])
            ins = v.tensor_tensor(stage, R["p0"][:], f[:, 0:512], ALU.mult)
            ev_outbank[0] = em.inc(ins, s["ev"], 1)
            em.wait("sync", s["ev"], ev_outbank[0])
            st_stage["st"] = em.inc(
                sy.dma_start(out=attn_cc[hh * 128:(hh + 1) * 128,
                                         qt * 512:(qt + 1) * 512], in_=stage),
                s["st"], 16)
    em.wait("sync", s["st"])
    em.wait("tensor", s["st"])


RG = [[0, 1, 2, 3], [4, 5, 6, 7]]


def emit_coll(p, em, R, kind, cc_in, cc_out, s, gap_fn=None):
    g = em.gpsimd
    em.wait("gpsimd", s["st"])
    em.wait("gpsimd", s["out"])
    ins = g.collective_compute(kind, ALU.bypass, replica_groups=RG,
                               ins=[cc_in.opt()], outs=[cc_out.opt()])
    em.inc(ins, s["cc"], 1)
    if gap_fn is not None:
        gap_fn()
    for e in ENGINES:
        em.wait(e, s["cc"])


def emit_attn_load_a(p, em, R, attn_full, tok0_hbm, s):
    import bass_rust
    g, sy = em.gpsimd, em.sync
    a = R["a"]
    em.inc(g.dma_start(out=R["ioff"][0:1, 0:1], in_=tok0_hbm), s["ld"], 16)
    em.wait("sync", s["ld"])
    em.wait("sync", s["mm"])
    em.wait("sync", s["ev"])
    if em.active == "sync":
        if "_tokval" not in R:
            reg = sy.alloc_register("tok0reg")
            sy.reg_load(reg, R["ioff"][0:1, 0:1])
            R["_tokval"] = bass_rust.RuntimeValue(reg, min_val=0,
                                                  max_val=SEQ - T)
        val = R["_tokval"]
    else:
        val = 0
    last = None
    for ach in range(DC):
        last = sy.dma_start(out=a[:, ach * T:(ach + 1) * T],
                            in_=attn_full[ach * 128:(ach + 1) * 128,
                                          bass.ds(val, T)])
        em.inc(last, s["ld"], 16)
    em.wait("tensor", s["ld"])
    em.wait("gpsimd", s["ld"])


def emit_outproj_res(p, em, R, wo_hbm, s, bo_hbm=None):
    a, w, c, cols, f = R["a"], R["w"], R["c"], R["cols"], R["f"]
    t, v, sc, sy, g = em.tensor, em.vector, em.scalar, em.sync, em.gpsimd
    ev_bank = {}
    em.wait("gpsimd", s["ev"])
    for oc4 in range(4):
        em.wait("gpsimd", s["mm"])
        ins = g.dma_start(out=w[:, 0:8192]
                          .rearrange("q (ach o) -> q ach o", o=512),
                          in_=wo_hbm[:, oc4 * 512:(oc4 + 1) * 512]
                          .rearrange("(ach q) o -> q ach o", q=128))
        em.inc(ins, s["wg"], 16)
        if bo_hbm is not None:
            ins = g.dma_start(out=R["r1"][0:1, 512:1024],
                              in_=bo_hbm[oc4 * 512:(oc4 + 1) * 512]
                              .rearrange("(x y) -> x y", x=1))
            em.inc(ins, s["wg"], 16)
            em.wait("vector", s["wg"])
            em.chain("vector", v.tensor_copy(f[0:1, 2560:3072],
                                             R["r1"][0:1, 512:1024]), s["ev"])
            em.wait("tensor", s["ev"])
            if "p3" in ev_bank:
                em.wait("tensor", s["ev"], ev_bank["p3"])
            ins = t.matmul(R["p3"][:], R["ones1f"][:], f[0:1, 2560:3072],
                           start=True, stop=True)
            bmt = em.inc(ins, s["mm"], 1)
            em.wait("vector", s["mm"], bmt)
            em.chain("vector", v.tensor_copy(f[:, 2560:3072], R["p3"][:]),
                     s["ev"])
            ev_bank["p3"] = s["ev"].n
        em.wait("tensor", s["wg"])
        for tt in range(TT):
            b = tt % 2
            pq = R[f"p{b}"]
            if b in ev_bank:
                em.wait("tensor", s["ev"], ev_bank[b])
            for ach in range(DC):
                ins = t.matmul(pq[:],
                               a[:, ach * T + tt * 128:ach * T + tt * 128 + 128],
                               w[:, ach * 512:(ach + 1) * 512],
                               start=(ach == 0), stop=(ach == DC - 1))
            mmt = em.inc(ins, s["mm"], 1)
            em.wait("vector", s["mm"], mmt)
            xt = R["x"][:, tt * D + oc4 * 512:tt * D + (oc4 + 1) * 512]
            if bo_hbm is not None:
                em.chain("vector", v.tensor_tensor(xt, xt, f[:, 2560:3072],
                                                   ALU.add), s["ev"])
            ins = v.tensor_tensor(xt, xt, pq[:], ALU.add)
            ev_bank[b] = em.inc(ins, s["ev"], 1)
    em.wait("vector", s["ev"])


MAGIC = 1.5 * 2 ** 23


def emit_ternarize(p, em, R, w_hbm, t_hbm, n_r, n_c, s, eng="vector"):
    c, cols, f = R["c"], R["cols"], R["f"]
    ve = em.eng(eng)
    sc, sy, t, v = em.scalar, em.sync, em.tensor, em.vector
    ntr = n_r // 128
    ncc = n_c // 2048
    ntile = ntr * ncc
    ld_tgt = []
    for i in range(ntile + 1):
        if i < ntile:
            if i >= 2:
                em.wait("sync", s["ev"])
            r0, c0 = (i // ncc) * 128, (i % ncc) * 2048
            ins = sy.dma_start(out=f[:, (i % 2) * 2048:(i % 2) * 2048 + 2048],
                               in_=w_hbm[r0:r0 + 128, c0:c0 + 2048])
            ld_tgt.append(em.inc(ins, s["ld"], 16))
        if i == 0:
            continue
        j = i - 1
        em.wait("scalar", s["ld"], ld_tgt[j])
        ins = sc.activation(c[:, 0:4096].bitcast(F32)[:, 0:2048],
                            f[:, (j % 2) * 2048:(j % 2) * 2048 + 2048],
                            AF.Abs, accum_out=cols[:, 64 + j % 32:65 + j % 32])
        em.inc(ins, s["ev"], 1)
        if (j + 1) % 32 == 0 or j == ntile - 1:
            em.wait("vector", s["ev"])
            nacc = (j % 32) + 1
            if j < 32:
                em.chain("vector",
                         v.tensor_reduce(cols[:, 0:1], cols[:, 64:64 + nacc],
                                         mybir.AxisListType.X, ALU.add), s["ev"])
            else:
                em.chain("vector",
                         v.tensor_reduce(cols[:, 1:2], cols[:, 64:64 + nacc],
                                         mybir.AxisListType.X, ALU.add), s["ev"])
                em.chain("vector",
                         v.tensor_tensor(cols[:, 0:1], cols[:, 0:1],
                                         cols[:, 1:2], ALU.add), s["ev"])
            em.wait("scalar", s["ev"])
    em.wait("tensor", s["ev"])
    ins = t.matmul(R["p7"][0:1, 0:1], R["onescf"][:, 0:1], cols[:, 0:1],
                   start=True, stop=True)
    em.inc(ins, s["mm"], 1)
    em.wait("vector", s["mm"])
    gam = cols[0:1, 8:9]
    em.chain("vector", v.tensor_scalar(gam, R["p7"][0:1, 0:1],
                                       1.0 / (n_r * n_c), None, ALU.mult),
             s["ev"])
    em.chain("vector", v.tensor_scalar(cols[0:1, 10:11], gam, EPS2, None,
                                       ALU.add), s["ev"])
    em.chain("vector", v.reciprocal(cols[0:1, 9:10], cols[0:1, 10:11]), s["ev"])
    em.wait("tensor", s["ev"])
    ins = t.matmul(R["p7"][:, 4:6], R["ones1f"][:], cols[0:1, 8:10],
                   start=True, stop=True)
    em.inc(ins, s["mm"], 1)
    em.wait("vector", s["mm"])
    gcol = cols[:, 12:13]
    rcol = cols[:, 13:14]
    em.chain("vector", v.tensor_copy(gcol, R["p7"][:, 4:5]), s["ev"])
    em.chain("vector", v.tensor_copy(rcol, R["p7"][:, 5:6]), s["ev"])
    em.wait(eng, s["ev"])
    em.wait("sync", s["ev"])
    ld_tgt = []
    st_tgt = {}
    for i in range(ntile + 1):
        if i < ntile:
            if i >= 2:
                em.wait("sync", s["ev"])
            r0, c0 = (i // ncc) * 128, (i % ncc) * 2048
            ins = sy.dma_start(out=f[:, (i % 2) * 2048:(i % 2) * 2048 + 2048],
                               in_=w_hbm[r0:r0 + 128, c0:c0 + 2048])
            ld_tgt.append(em.inc(ins, s["ld"], 16))
        if i == 0:
            continue
        j = i - 1
        b = j % 2
        em.wait(eng, s["ld"], ld_tgt[j])
        if b in st_tgt:
            em.wait(eng, s["st"], st_tgt[b])
        wt = f[:, b * 2048:b * 2048 + 2048]
        stg = c[:, b * 2048:(b + 1) * 2048]
        em.chain(eng, ve.tensor_scalar(wt, wt, rcol, MAGIC, ALU.mult, ALU.add),
                 s["ev"])
        em.chain(eng, ve.tensor_scalar(wt, wt, MAGIC, 1.0, ALU.subtract,
                                       ALU.min), s["ev"])
        em.chain(eng, ve.tensor_scalar(stg, wt, -1.0, gcol, ALU.max, ALU.mult),
                 s["ev"])
        em.wait("sync", s["ev"])
        r0, c0 = (j // ncc) * 128, (j % ncc) * 2048
        st_tgt[b] = em.inc(sy.dma_start(out=t_hbm[r0:r0 + 128, c0:c0 + 2048],
                                        in_=stg), s["st"], 16)
    em.wait("sync", s["st"])
    em.wait(eng, s["st"])


def emit_ffn1(p, em, R, tfc_hbm, tproj_hbm, h_hbm, h2_hbm, s):
    a, w, c, cols, f = R["a"], R["w"], R["c"], R["cols"], R["f"]
    t, v, sc, sy, g = em.tensor, em.vector, em.scalar, em.sync, em.gpsimd
    NOC = 16
    ev_bank = {}
    st_stage = {}
    for oc in range(NOC):
        em.wait("sync", s["mm"])
        ins = sy.dma_start(out=w[:, 0:8192].rearrange("q (dch o) -> q dch o",
                                                      o=512),
                           in_=tfc_hbm[:, oc * 512:(oc + 1) * 512]
                           .rearrange("(dch q) o -> q dch o", q=128))
        em.inc(ins, s["wg"], 16)
        em.wait("tensor", s["wg"])
        for tt in range(TT):
            b = tt % 2
            pq = R[f"p{b}"]
            if b in ev_bank:
                em.wait("tensor", s["ev"], ev_bank[b])
            for dch in range(DC):
                ins = t.matmul(pq[:],
                               a[:, dch * T + tt * 128:dch * T + tt * 128 + 128],
                               w[:, dch * 512:(dch + 1) * 512],
                               start=(dch == 0), stop=(dch == DC - 1))
            mmt = em.inc(ins, s["mm"], 1)
            em.wait("scalar", s["mm"], mmt)
            stage = c[:, b * 512:b * 512 + 512]
            if b in st_stage:
                em.wait("scalar", s["st"], st_stage[b])
            i2 = sc.activation(stage, pq[:], AF.Gelu,
                               accum_out=cols[:, tt * 16 + oc:tt * 16 + oc + 1])
            evt = em.inc(i2, s["ev"], 1)
            em.wait("vector", s["ev"], evt)
            i3 = v.scalar_tensor_tensor(
                c[:, 2048 + b * 512:2048 + b * 512 + 512], stage, 1.0, stage,
                ALU.mult, ALU.mult,
                accum_out=cols[:, 256 + tt * 16 + oc:257 + tt * 16 + oc])
            evt = em.inc(i3, s["ev"], 1)
            ev_bank[b] = evt
            em.wait("sync", s["ev"], evt)
            st_stage[b] = em.inc(
                sy.dma_start(out=h_hbm[tt * 128:(tt + 1) * 128,
                                       oc * 512:(oc + 1) * 512], in_=stage),
                s["st"], 16)
    em.wait("sync", s["st"])
    em.wait("vector", s["st"])
    n8 = 1.0 / 8192.0
    for tt in range(TT):
        cb = 16 * tt
        em.wait("sync", s["st"])
        ins = sy.dma_start(out=c[:, 0:8192], in_=h_hbm[tt * 128:(tt + 1) * 128, :])
        tld = em.inc(ins, s["ld"], 16)
        em.wait("vector", s["ld"], tld)
        sm = cols[:, 0:1]
        sq = cols[:, 1:2]
        em.chain("vector", v.tensor_reduce(sm, cols[:, cb:cb + 16],
                                           mybir.AxisListType.X, ALU.add),
                 s["ev"])
        em.chain("vector", v.tensor_reduce(sq, cols[:, 256 + cb:256 + cb + 16],
                                           mybir.AxisListType.X, ALU.add),
                 s["ev"])
        nm = cols[:, 2:3]
        nm2 = cols[:, 3:4]
        var = cols[:, 4:5]
        srt = cols[:, 5:6]
        rstd = cols[:, 6:7]
        em.chain("vector", v.tensor_scalar(nm, sm, -n8, None, ALU.mult), s["ev"])
        em.chain("vector", v.tensor_tensor(nm2, nm, nm, ALU.mult), s["ev"])
        em.inc(v.scalar_tensor_tensor(var, sq, n8, nm2, ALU.mult, ALU.subtract),
               s["ev"], 1)
        em.wait("scalar", s["ev"])
        em.inc(sc.activation(srt, var, AF.Sqrt, bias=R["eps"][:, 1:2]), s["ev"], 1)
        em.wait("vector", s["ev"])
        em.chain("vector", v.reciprocal(rstd, srt), s["ev"])
        hmx = cols[:, 7:8]
        hmn = cols[:, 8:9]
        t1 = cols[:, 9:10]
        t2 = cols[:, 10:11]
        gam = cols[:, 11:12]
        gneg = cols[:, 12:13]
        em.chain("vector", v.tensor_reduce(hmx, c[:, 0:8192],
                                           mybir.AxisListType.X, ALU.max),
                 s["ev"])
        em.chain("vector", v.tensor_reduce(hmn, c[:, 0:8192],
                                           mybir.AxisListType.X, ALU.min),
                 s["ev"])
        em.chain("vector", v.tensor_tensor(t1, hmx, nm, ALU.add), s["ev"])
        em.chain("vector", v.tensor_tensor(t2, hmn, nm, ALU.add), s["ev"])
        em.chain("vector", v.tensor_scalar(t2, t2, -1.0, None, ALU.mult), s["ev"])
        em.chain("vector", v.tensor_tensor(gam, t1, t2, ALU.max), s["ev"])
        em.chain("vector", v.tensor_tensor(gam, gam, rstd, ALU.mult), s["ev"])
        em.chain("vector", v.tensor_scalar(gam, gam, CLIPF, None, ALU.mult),
                 s["ev"])
        em.chain("vector", v.tensor_scalar(gneg, gam, -1.0, None, ALU.mult),
                 s["ev"])
        for half in range(2):
            hs = c[:, half * 4096:(half + 1) * 4096]
            tmp = f[:, 0:4096]
            em.chain("vector", v.tensor_scalar(tmp, hs, nm, rstd, ALU.add,
                                               ALU.mult), s["ev"])
            em.chain("vector", v.tensor_scalar(hs, tmp, gam, gneg, ALU.min,
                                               ALU.max), s["ev"])
            em.wait("sync", s["ev"])
            em.inc(sy.dma_start(out=h2_hbm[tt * 128:(tt + 1) * 128,
                                           half * 4096:(half + 1) * 4096],
                                in_=hs), s["st"], 16)
    em.wait("sync", s["st"])
    NICH = 64
    for dq in range(4):
        em.wait("gpsimd", s["mm"])
        em.wait("gpsimd", s["ev"])
        ins = g.dma_start(out=a[:, 0:32768].rearrange("q (ich o) -> q ich o",
                                                      o=512),
                          in_=tproj_hbm[:, dq * 512:(dq + 1) * 512]
                          .rearrange("(ich q) o -> q ich o", q=128))
        em.inc(ins, s["wg"], 16)
        em.wait("tensor", s["wg"])
        for tg in range(2):
            mm_base = s["mm"].n
            em.wait("tensor", s["ev"])
            for ib in range(16):
                hb = ib % 2
                if ib >= 2:
                    em.wait("sync", s["mm"], mm_base + ib - 1)
                else:
                    em.wait("sync", s["mm"], mm_base)
                for k in range(4):
                    ich = ib * 4 + k
                    ins = sy.dma_start(
                        out=c[:, hb * 4096 + k * 1024:hb * 4096 + (k + 1) * 1024],
                        in_=h2_hbm[tg * 1024:(tg + 1) * 1024,
                                   ich * 128:(ich + 1) * 128], transpose=True)
                    em.inc(ins, s["tr"], 16)
                em.wait("tensor", s["tr"])
                for k in range(4):
                    ich = ib * 4 + k
                    for ttl in range(8):
                        ins = t.matmul(
                            R[f"p{ttl}"][:],
                            c[:, hb * 4096 + k * 1024 + ttl * 128:
                              hb * 4096 + k * 1024 + ttl * 128 + 128],
                            a[:, ich * 512:(ich + 1) * 512],
                            start=(ich == 0), stop=(ich == NICH - 1))
                ins.then_inc(s["mm"].h, 1)
                s["mm"].n += 1
            mmt = s["mm"].n
            em.wait("vector", s["mm"], mmt)
            for ttl in range(8):
                tt = tg * 8 + ttl
                xt = R["x"][:, tt * D + dq * 512:tt * D + (dq + 1) * 512]
                ins = v.scalar_tensor_tensor(xt, R[f"p{ttl}"][:], 0.5, xt,
                                             ALU.mult, ALU.add)
                em.inc(ins, s["ev"], 1)
    em.wait("vector", s["ev"])



MODE = "fused"

F16 = mybir.dt.float16
I = 5632


def shard_inputs(inp):
    x = np.asarray(inp["L0_attn_hidden_states"])
    l0_wq = np.asarray(inp["L0_attn_w_qkv"])
    l0_wo = np.asarray(inp["L0_attn_w_o"])
    gu_pk = np.asarray(inp["L0_ffn_gate_up_packed"])
    gu_sc = np.asarray(inp["L0_ffn_gate_up_scales"])
    dn_pk = np.asarray(inp["L0_ffn_down_packed"])
    dn_sc = np.asarray(inp["L0_ffn_down_scales"])
    l1_wq = np.asarray(inp["L1_attn_Wqkv"])
    l1_bq = np.asarray(inp["L1_attn_bqkv"])
    l1_wo = np.asarray(inp["L1_attn_Wo"])
    l1_bo = np.asarray(inp["L1_attn_bo"])
    wfc = np.asarray(inp["L1_ffn_w_fc"])
    wproj = np.asarray(inp["L1_ffn_w_proj"])

    C = np.ascontiguousarray
    l0_wqT = C(l0_wq.T)
    l0_woT = C(l0_wo.T)
    wfcT = C(wfc.T)
    wprojT = C(wproj.T)
    scT_gu = C(gu_sc.T)
    scT_dn = C(dn_sc.T)
    slopes_all = np.array([0.5 ** (k / 2) for k in range(NH)], np.float32)

    def qkv_slice(wT, r):
        return C(np.concatenate([wT[:, r * 512:(r + 1) * 512],
                                 wT[:, D + r * 512:D + (r + 1) * 512],
                                 wT[:, 2 * D + r * 512:2 * D + (r + 1) * 512]],
                                axis=1))

    def b_slice(b, r):
        return C(np.concatenate([b[r * 512:(r + 1) * 512],
                                 b[D + r * 512:D + (r + 1) * 512],
                                 b[2 * D + r * 512:2 * D + (r + 1) * 512]]))

    per_core = []
    for core in range(8):
        b, r = core // 4, core % 4
        per_core.append({
            "x_in": C(x[b, r * 2048:(r + 1) * 2048]),
            "pk_gu": gu_pk, "scT_gu": scT_gu, "pk_dn": dn_pk, "scT_dn": scT_dn,
            "l0_wq": qkv_slice(l0_wqT, r), "l0_wo": l0_woT,
            "l1_wq": qkv_slice(l1_wq, r), "l1_bq": b_slice(l1_bq, r),
            "l1_wo": C(l1_wo), "l1_bo": C(l1_bo),
            "wfcT": wfcT, "wprojT": wprojT,
            "slopes": C(slopes_all[r * 4:(r + 1) * 4]),
            "tok0": np.array([[r * 2048]], np.int32),
        })
    return per_core


def emit_attn_subl(p, em, R, s, dr, l0, gap1=None, gap2=None):
    emit_ln_to_aT(p, em, R, dr["xp"], s, mode="ln", to_aT=False)
    barrier(em, s["bar"])
    emit_coll(p, em, R, "AllGather", dr["xp"], dr["xn_full"], s, gap_fn=gap1)
    if l0:
        emit_qkv_h(p, em, R, dr["l0_wq"], dr["xn_full"], dr["q_hbm"],
                   dr["k_hbm"], dr["v_hbm"], dr["qn_hbm"], dr["kn_hbm"], s)
    else:
        emit_qkv_h(p, em, R, dr["l1_wq"], dr["xn_full"], dr["q_hbm"],
                   dr["k_hbm"], dr["v_hbm"], dr["qn_hbm"], dr["kn_hbm"], s,
                   bias_hbm=dr["l1_bq"])
    barrier(em, s["bar"])
    emit_attn_masks(p, em, R, s)
    barrier(em, s["bar"])
    emit_attn_flash_h(p, em, R, dr["q_hbm"], dr["k_hbm"], dr["v_hbm"],
                      dr["qn_hbm"], dr["kn_hbm"], dr["attn_cc"], s,
                      not l0, slopes_hbm=None if l0 else dr["slopes"])
    barrier(em, s["bar"])
    emit_coll(p, em, R, "AllGather", dr["attn_cc"], dr["attn_full"], s,
              gap_fn=gap2)
    emit_attn_load_a(p, em, R, dr["attn_full"], dr["tok0"], s)
    barrier(em, s["bar"])
    if l0:
        emit_outproj_res(p, em, R, dr["l0_wo"], s)
    else:
        emit_outproj_res(p, em, R, dr["l1_wo"], s, bo_hbm=dr["l1_bo"])


def emit_ffn0_subl(p, em, R, s, dr):
    emit_ln_to_aT(p, em, R, dr["xp"], s, mode="ln")
    barrier(em, s["bar"])
    emit_ffn0(p, em, R, dr["wgu_dq"], dr["wdn_dq"], dr["h0_hbm"],
              dr["y_hbm"], s)
    barrier(em, s["bar"])
    emit_rms_res(p, em, R, dr["y_hbm"], s, scale=0.5)


def emit_ffn1_subl(p, em, R, s, dr):
    emit_ln_to_aT(p, em, R, dr["xp"], s, mode="bitlin")
    barrier(em, s["bar"])
    emit_ffn1(p, em, R, dr["tfc"], dr["tproj"], dr["h1_hbm"], dr["h2_hbm"], s)


def decl_common_scratch(p, dr):
    dr["xp"] = p.dram("xp", [T, D], BF16)
    dr["xn_full"] = p.dram("xn_full", [SEQ, D], BF16)
    dr["q_hbm"] = p.dram("q_hbm", [512, SEQ], BF16)
    dr["k_hbm"] = p.dram("k_hbm", [512, SEQ], BF16)
    dr["v_hbm"] = p.dram("v_hbm", [SEQ, 512], BF16)
    dr["qn_hbm"] = p.dram("qn_hbm", [4, SEQ], F32)
    dr["kn_hbm"] = p.dram("kn_hbm", [4, SEQ], F32)
    dr["attn_cc"] = p.dram("attn_cc", [512, SEQ], BF16)
    dr["attn_full"] = p.dram("attn_full", [D, SEQ], BF16)
    dr["h0_hbm"] = p.dram("h0_hbm", [T, I], BF16)
    dr["y_hbm"] = p.dram("y_hbm", [T, D], BF16)
    dr["h1_hbm"] = p.dram("h1_hbm", [T, 4 * D], BF16)
    dr["h2_hbm"] = p.dram("h2_hbm", [T, 4 * D], BF16)


def decl_weights(p, dr, which):
    if "l0ffn_in" in which:
        dr["pk_gu"] = p.dram("pk_gu", [2 * I, 256], I32, kind="ExternalInput")
        dr["scT_gu"] = p.dram("scT_gu", [2 * I, 16], F16, kind="ExternalInput")
        dr["pk_dn"] = p.dram("pk_dn", [D, 704], I32, kind="ExternalInput")
        dr["scT_dn"] = p.dram("scT_dn", [D, 44], F16, kind="ExternalInput")
    if "l1ffn_in" in which:
        dr["wfcT"] = p.dram("wfcT", [D, 4 * D], F32, kind="ExternalInput")
        dr["wprojT"] = p.dram("wprojT", [4 * D, D], F32, kind="ExternalInput")
    if "l0dq" in which:
        kind = "ExternalOutput" if "dqout" in which else (
            "ExternalInput" if "dqin" in which else "Internal")
        dr["wgu_dq"] = p.dram("wgu_dq", [2 * I, D], BF16, kind=kind)
        dr["wdn_dq"] = p.dram("wdn_dq", [D, I], BF16, kind=kind)
    if "l1t" in which:
        kind = "ExternalOutput" if "dqout" in which else (
            "ExternalInput" if "dqin" in which else "Internal")
        dr["tfc"] = p.dram("tfc", [D, 4 * D], BF16, kind=kind)
        dr["tproj"] = p.dram("tproj", [4 * D, D], BF16, kind=kind)
    if "l0attn" in which:
        dr["l0_wq"] = p.dram("l0_wq", [D, 1536], F32, kind="ExternalInput")
        dr["l0_wo"] = p.dram("l0_wo", [D, D], F32, kind="ExternalInput")
    if "l1attn" in which:
        dr["l1_wq"] = p.dram("l1_wq", [D, 1536], F32, kind="ExternalInput")
        dr["l1_bq"] = p.dram("l1_bq", [1536], F32, kind="ExternalInput")
        dr["l1_wo"] = p.dram("l1_wo", [D, D], F32, kind="ExternalInput")
        dr["l1_bo"] = p.dram("l1_bo", [D], F32, kind="ExternalInput")
        dr["slopes"] = p.dram("slopes", [4], F32, kind="ExternalInput")
    if "attn" in which or "l0attn" in which or "l1attn" in which:
        dr["tok0"] = p.dram("tok0", [1, 1], I32, kind="ExternalInput")


def build_fused():
    p = Prog()
    dr = {}
    dr["x_in"] = p.dram("x_in", [T, D], F32, kind="ExternalInput")
    dr["x_out"] = p.dram("x_out", [T, D], F32, kind="ExternalOutput")
    decl_weights(p, dr, ("l0ffn_in", "l1ffn_in", "l0dq", "l1t", "l0attn",
                         "l1attn"))
    decl_common_scratch(p, dr)
    R = alloc_common(p)
    s = std_sems(p)

    def body(em):
        emit_const_init(p, em, R, s)
        emit_load_x(p, em, R, dr["x_in"], s)
        emit_dequant(p, em, R, dr["pk_gu"], dr["scT_gu"], dr["wgu_dq"],
                     2 * I, 256, s)
        barrier(em, s["bar"])
        emit_dequant(p, em, R, dr["pk_dn"], dr["scT_dn"], dr["wdn_dq"],
                     D, 704, s)
        barrier(em, s["bar"])
        emit_ternarize(p, em, R, dr["wfcT"], dr["tfc"], D, 4 * D, s)
        barrier(em, s["bar"])
        emit_ternarize(p, em, R, dr["wprojT"], dr["tproj"], 4 * D, D, s)
        barrier(em, s["bar"])
        emit_ffn0_subl(p, em, R, s, dr)
        barrier(em, s["bar"])
        emit_attn_subl(p, em, R, s, dr, l0=True)
        barrier(em, s["bar"])
        emit_ffn0_subl(p, em, R, s, dr)
        barrier(em, s["bar"])
        emit_ffn1_subl(p, em, R, s, dr)
        barrier(em, s["bar"])
        emit_attn_subl(p, em, R, s, dr, l0=False)
        barrier(em, s["bar"])
        emit_ffn1_subl(p, em, R, s, dr)
        barrier(em, s["bar"])
        emit_store_x(p, em, R, dr["x_out"], s)
        em.wait("sync", s["out"])
        em.wait("gpsimd", s["out"])

    p.emit(body)
    return p.nc


def build_prep():
    p = Prog()
    dr = {}
    decl_weights(p, dr, ("l0ffn_in", "l1ffn_in", "l0dq", "l1t", "dqout"))
    R = alloc_common(p)
    s = std_sems(p)

    def body(em):
        emit_const_init(p, em, R, s)
        emit_dequant(p, em, R, dr["pk_gu"], dr["scT_gu"], dr["wgu_dq"],
                     2 * I, 256, s)
        barrier(em, s["bar"])
        emit_dequant(p, em, R, dr["pk_dn"], dr["scT_dn"], dr["wdn_dq"],
                     D, 704, s)
        barrier(em, s["bar"])
        emit_ternarize(p, em, R, dr["wfcT"], dr["tfc"], D, 4 * D, s)
        barrier(em, s["bar"])
        emit_ternarize(p, em, R, dr["wprojT"], dr["tproj"], 4 * D, D, s)
        em.wait("sync", s["st"])
        em.wait("gpsimd", s["st"])

    p.emit(body)
    return p.nc


def build_subl(which):
    p = Prog()
    dr = {}
    dr["x_in"] = p.dram("x_in", [T, D], F32, kind="ExternalInput")
    dr["x_out"] = p.dram("x_out", [T, D], F32, kind="ExternalOutput")
    decl_common_scratch(p, dr)
    if which == "ffn0":
        decl_weights(p, dr, ("l0dq", "dqin"))
    elif which == "ffn1":
        decl_weights(p, dr, ("l1t", "dqin"))
    elif which == "attn0":
        decl_weights(p, dr, ("l0attn",))
    elif which == "attn1":
        decl_weights(p, dr, ("l1attn",))
    R = alloc_common(p)
    s = std_sems(p)

    def body(em):
        emit_const_init(p, em, R, s)
        emit_load_x(p, em, R, dr["x_in"], s)
        if which == "ffn0":
            emit_ffn0_subl(p, em, R, s, dr)
        elif which == "ffn1":
            emit_ffn1_subl(p, em, R, s, dr)
        elif which == "attn0":
            emit_attn_subl(p, em, R, s, dr, l0=True)
        else:
            emit_attn_subl(p, em, R, s, dr, l0=False)
        barrier(em, s["bar"])
        emit_store_x(p, em, R, dr["x_out"], s)
        em.wait("sync", s["out"])
        em.wait("gpsimd", s["out"])

    p.emit(body)
    return p.nc


_CACHE = {}

_INPUTS = {
    "prep": ["pk_gu", "scT_gu", "pk_dn", "scT_dn", "wfcT", "wprojT"],
    "ffn0": ["x_in", "wgu_dq", "wdn_dq"],
    "ffn1": ["x_in", "tfc", "tproj"],
    "attn0": ["x_in", "l0_wq", "l0_wo", "tok0"],
    "attn1": ["x_in", "l1_wq", "l1_bq", "l1_wo", "l1_bo", "slopes", "tok0"],
    "fused": ["x_in", "pk_gu", "scT_gu", "pk_dn", "scT_dn", "wfcT", "wprojT",
              "l0_wq", "l0_wo", "l1_wq", "l1_bq", "l1_wo", "l1_bo", "slopes",
              "tok0"],
}


def _get(name, fn):
    if name not in _CACHE:
        _CACHE[name] = fn()
    return _CACHE[name]


def kernel(**inputs):
    per_core = shard_inputs(inputs)
    cores = list(range(8))
    if MODE == "fused":
        nc = _get("fused", build_fused)
        maps = [{k: pc[k] for k in _INPUTS["fused"]} for pc in per_core]
        res = run_bass_kernel_spmd(nc, maps, core_ids=cores)
        xs = [r["x_out"] for r in res.results]
    else:
        nc = _get("prep", build_prep)
        maps = [{k: pc[k] for k in _INPUTS["prep"]} for pc in per_core]
        res = run_bass_kernel_spmd(nc, maps, core_ids=cores)
        for c in range(8):
            for k in ("wgu_dq", "wdn_dq", "tfc", "tproj"):
                per_core[c][k] = res.results[c][k]
        seqs = ["ffn0", "attn0", "ffn0", "ffn1", "attn1", "ffn1"]
        xs = [pc["x_in"] for pc in per_core]
        for stage in seqs:
            nc = _get(stage, lambda w=stage: build_subl(w))
            for c in range(8):
                per_core[c]["x_in"] = xs[c]
            maps = [{k: pc[k] for k in _INPUTS[stage]} for pc in per_core]
            res = run_bass_kernel_spmd(nc, maps, core_ids=cores)
            xs = [r["x_out"] for r in res.results]
    out = np.zeros((2, SEQ, D), np.float32)
    for core in range(8):
        b, r = core // 4, core % 4
        out[b, r * 2048:(r + 1) * 2048] = xs[core]
    return out


# version 15 (final)
# speedup vs baseline: 1.0101x; 1.0101x over previous
import contextlib
import numpy as np
import concourse.bass as bass
import concourse.mybir as mybir
from concourse.bass_utils import run_bass_kernel_spmd

F32 = mybir.dt.float32
BF16 = mybir.dt.bfloat16
I32 = mybir.dt.int32
AF = mybir.ActivationFunctionType
ALU = mybir.AluOpType

D = 2048
T = 2048
TT = 16
DC = 16
NH = 16
HD = 128
SEQ = 8192
NCHUNK = 8
RSD = 1.0 / np.sqrt(128.0)

ENGINES = ("tensor", "vector", "scalar", "gpsimd", "sync")


class _Dummy:
    def __call__(self, *a, **k):
        return self
    def __getattr__(self, name):
        return self


_DUMMY = _Dummy()


class Sem:
    def __init__(self, handle, name):
        self.h = handle
        self.name = name
        self.n = 0


class Em:
    def __init__(self, nc, active, engine_obj, sems):
        self.nc = nc
        self.active = active
        self._eng = engine_obj
        self.sems = sems

    def __getattr__(self, name):
        if name in ENGINES:
            return self._eng if name == self.active else _DUMMY
        raise AttributeError(name)

    def eng(self, name):
        return self._eng if name == self.active else _DUMMY

    def inc(self, instr, sem, k=1):
        sem.n += k
        instr.then_inc(sem.h, k)
        return sem.n

    def wait(self, engine_name, sem, val=None):
        self.eng(engine_name).wait_ge(sem.h, sem.n if val is None else val)

    def chain(self, engine_name, instr, sem):
        sem.n += 1
        instr.then_inc(sem.h, 1)
        self.eng(engine_name).wait_ge(sem.h, sem.n)


class Prog:
    def __init__(self, name="prog"):
        self.nc = bass.Bass("TRN2", target_bir_lowering=False)
        self.ctx = contextlib.ExitStack()
        self.name = name
        self._sb = {}
        self._ps = {}
        self._sems = {}
        self._dram = {}

    def dram(self, name, shape, dtype, kind="Internal"):
        ap = self.nc.dram_tensor(name, list(shape), dtype, kind=kind).ap()
        self._dram[name] = ap
        return ap

    def sb(self, name, shape, dtype):
        t = self.ctx.enter_context(self.nc.sbuf_tensor(name, list(shape), dtype))
        self._sb[name] = t
        return t

    def ps(self, name, shape, dtype=F32):
        t = self.ctx.enter_context(self.nc.psum_tensor(name, list(shape), dtype))
        self._ps[name] = t
        return t

    def sem(self, name):
        h = self.ctx.enter_context(self.nc.semaphore(name))
        s = Sem(h, name)
        self._sems[name] = s
        return s

    def emit(self, body):
        block = self.ctx.enter_context(self.nc.Block())
        saved = {n: s.n for n, s in self._sems.items()}

        def run_pass(engine_name, engine_obj):
            for n, s in self._sems.items():
                s.n = saved[n]
            body(Em(self.nc, engine_name, engine_obj, self._sems))

        @block.tensor
        def _(t):
            run_pass("tensor", t)

        @block.vector
        def _(v):
            run_pass("vector", v)

        @block.scalar
        def _(s):
            run_pass("scalar", s)

        @block.gpsimd
        def _(g):
            run_pass("gpsimd", g)

        @block.sync
        def _(sy):
            run_pass("sync", sy)

        self.ctx.close()
        return self.nc




EPS1 = 1e-5
EPS2 = 1e-6
CLIPF = (128.0 - 1e-6) / 128.0


def alloc_common(p):
    R = {}
    R["x"] = p.sb("x_sb", [128, TT * D], BF16)
    R["a"] = p.sb("a_sb", [128, 32768], BF16)
    R["w"] = p.sb("w_sb", [128, 16384], BF16)
    R["c"] = p.sb("c_sb", [128, 8192], BF16)
    R["f"] = p.sb("f_sb", [128, 4096], F32)
    R["cols"] = p.sb("cols_sb", [128, 512], F32)
    R["r1"] = p.sb("r1_sb", [1, 1024], F32)
    R["r2"] = p.sb("r2_sb", [1, 2048], BF16)
    R["ones1"] = p.sb("ones1_sb", [1, 128], BF16)
    R["onesc"] = p.sb("onesc_sb", [128, 4], BF16)
    R["ones1f"] = p.sb("ones1f_sb", [1, 128], F32)
    R["onescf"] = p.sb("onescf_sb", [128, 4], F32)
    R["zero"] = p.sb("zero_sb", [128, 512], BF16)
    R["eps"] = p.sb("eps_sb", [128, 2], F32)
    R["ioff"] = p.sb("ioff_sb", [1, 2], I32)
    for i in range(8):
        R[f"p{i}"] = p.ps(f"p{i}", [128, 512], F32)
    return R


def std_sems(p):
    return {
        "init": p.sem("init"),
        "ld": p.sem("ld"),
        "ev": p.sem("ev"),
        "st": p.sem("st"),
        "tr": p.sem("tr"),
        "mm": p.sem("mm"),
        "wg": p.sem("wg"),
        "cc": p.sem("cc"),
        "out": p.sem("out"),
        "bar": p.sem("bar"),
        "ld2": p.sem("ld2"),
        "st2": p.sem("st2"),
        "ev2": p.sem("ev2"),
    }


def emit_const_init(p, em, R, s):
    g = em.gpsimd
    g.memset(R["ones1"][:], 1.0)
    g.memset(R["onesc"][:], 1.0)
    g.memset(R["ones1f"][:], 1.0)
    g.memset(R["onescf"][:], 1.0)
    g.memset(R["eps"][:, 0:1], EPS1)
    g.memset(R["eps"][:, 1:2], EPS2)
    em.inc(g.memset(R["zero"][:], 0.0), s["init"], 1)
    for e in ENGINES:
        em.wait(e, s["init"])


def barrier(em, bar):
    for e in ENGINES:
        em.eng(e).nop().then_inc(bar.h, 1)
    bar.n += 5
    for e in ENGINES:
        em.eng(e).wait_ge(bar.h, bar.n)


def emit_load_x(p, em, R, x_hbm, s):
    g = em.gpsimd
    x = R["x"]
    for t in range(TT):
        em.inc(g.dma_start(out=x[:, t * D:(t + 1) * D],
                           in_=x_hbm[t * 128:(t + 1) * 128, :]), s["ld"], 16)
    for e in ("vector", "scalar", "tensor", "sync"):
        em.wait(e, s["ld"])


def emit_store_x(p, em, R, xo_hbm, s):
    g = em.gpsimd
    x = R["x"]
    for t in range(TT):
        em.inc(g.dma_start(out=xo_hbm[t * 128:(t + 1) * 128, :],
                           in_=x[:, t * D:(t + 1) * D]), s["out"], 16)
    em.wait("gpsimd", s["out"])


def emit_ln_stats(p, em, R, s, mode="ln", src=None, ncols=TT, width=D,
                  eps1=EPS1):
    cols, f = R["cols"], R["f"]
    x = R["x"]
    if src is None:
        src = lambda t: x[:, t * D:(t + 1) * D]
    v, sc = em.vector, em.scalar
    n = 1.0 / width
    S = cols[:, 0:ncols]
    Q = cols[:, 16:16 + ncols]
    M = cols[:, 32:32 + ncols]
    N = cols[:, 48:48 + ncols]
    for t in range(ncols):
        xt = src(t)
        i1 = v.tensor_reduce(S[:, t:t + 1], xt, mybir.AxisListType.X, ALU.add)
        i2 = sc.activation(f[:, 0:2048], xt, AF.Square, accum_out=Q[:, t:t + 1])
        if mode == "bitlin":
            v.tensor_reduce(M[:, t:t + 1], xt, mybir.AxisListType.X, ALU.max)
            i1 = v.tensor_reduce(N[:, t:t + 1], xt, mybir.AxisListType.X, ALU.min)
        if t == ncols - 1:
            em.inc(i1, s["ev"], 1)
            em.inc(i2, s["ev"], 1)
    em.wait("vector", s["ev"])
    nm = cols[:, 64:64 + ncols]
    nm2 = cols[:, 160:160 + ncols]
    var = cols[:, 176:176 + ncols]
    srt = cols[:, 192:192 + ncols]
    rstd = cols[:, 80:80 + ncols]
    em.chain("vector", v.tensor_scalar(nm, S, -n, None, ALU.mult), s["ev"])
    em.chain("vector", v.tensor_tensor(nm2, nm, nm, ALU.mult), s["ev"])
    em.inc(v.scalar_tensor_tensor(var, Q, n, nm2, ALU.mult, ALU.subtract),
           s["ev"], 1)
    em.wait("scalar", s["ev"])
    epsc = R["eps"][:, 0:1] if eps1 == EPS1 else R["eps"][:, 1:2]
    em.inc(sc.activation(srt, var, AF.Sqrt, bias=epsc), s["ev"], 1)
    em.wait("vector", s["ev"])
    em.chain("vector", v.reciprocal(rstd, srt), s["ev"])
    if mode == "bitlin":
        A = cols[:, 96:96 + ncols]
        boff = cols[:, 112:112 + ncols]
        gam = cols[:, 128:128 + ncols]
        gneg = cols[:, 144:144 + ncols]
        v2 = cols[:, 208:208 + ncols]
        r2s = cols[:, 224:224 + ncols]
        r2 = cols[:, 240:240 + ncols]
        t1 = cols[:, 256:256 + ncols]
        t2 = cols[:, 272:272 + ncols]
        em.chain("vector", v.tensor_tensor(v2, var, rstd, ALU.mult), s["ev"])
        em.inc(v.tensor_tensor(v2, v2, rstd, ALU.mult), s["ev"], 1)
        em.wait("scalar", s["ev"])
        em.inc(sc.activation(r2s, v2, AF.Sqrt, bias=R["eps"][:, 1:2]), s["ev"], 1)
        em.wait("vector", s["ev"])
        em.chain("vector", v.reciprocal(r2, r2s), s["ev"])
        em.chain("vector", v.tensor_tensor(A, rstd, r2, ALU.mult), s["ev"])
        em.chain("vector", v.tensor_tensor(t1, M, nm, ALU.add), s["ev"])
        em.chain("vector", v.tensor_tensor(t2, N, nm, ALU.add), s["ev"])
        em.chain("vector", v.tensor_scalar(t2, t2, -1.0, None, ALU.mult), s["ev"])
        em.chain("vector", v.tensor_tensor(gam, t1, t2, ALU.max), s["ev"])
        em.chain("vector", v.tensor_tensor(gam, gam, A, ALU.mult), s["ev"])
        em.chain("vector", v.tensor_scalar(gam, gam, CLIPF, None, ALU.mult),
                 s["ev"])
        em.chain("vector", v.tensor_scalar(gneg, gam, -1.0, None, ALU.mult),
                 s["ev"])
        em.chain("vector", v.tensor_tensor(boff, nm, A, ALU.mult), s["ev"])


def emit_ln_to_aT(p, em, R, xp_hbm, s, mode="ln", to_aT=True):
    x, a, c, cols, f = R["x"], R["a"], R["c"], R["cols"], R["f"]
    v, sy = em.vector, em.sync
    emit_ln_stats(p, em, R, s, mode=mode)
    nm = cols[:, 64:80]
    rstd = cols[:, 80:96]
    A = cols[:, 96:112]
    boff = cols[:, 112:128]
    gam = cols[:, 128:144]
    gneg = cols[:, 144:160]
    store_tgt = []
    for t in range(TT):
        xt = x[:, t * D:(t + 1) * D]
        stage = c[:, (t % 2) * D:(t % 2) * D + D]
        if t >= 2:
            em.wait("vector", s["st"], store_tgt[t - 2])
        if mode == "ln":
            i = v.tensor_scalar(stage, xt, nm[:, t:t + 1], rstd[:, t:t + 1],
                                ALU.add, ALU.mult)
        else:
            tmp = f[:, 2048:4096]
            em.chain("vector",
                     v.tensor_scalar(tmp, xt, A[:, t:t + 1], boff[:, t:t + 1],
                                     ALU.mult, ALU.add), s["ev"])
            i = v.tensor_scalar(stage, tmp, gam[:, t:t + 1], gneg[:, t:t + 1],
                                ALU.min, ALU.max)
        em.inc(i, s["ev"], 1)
        em.wait("sync", s["ev"])
        tgt = em.inc(sy.dma_start(out=xp_hbm[t * 128:(t + 1) * 128, :], in_=stage),
                     s["st"], 16)
        store_tgt.append(tgt)
    em.wait("sync", s["st"])
    if to_aT:
        for dch in range(DC):
            em.inc(sy.dma_start(out=a[:, dch * T:(dch + 1) * T],
                                in_=xp_hbm[:, dch * 128:(dch + 1) * 128],
                                transpose=True), s["tr"], 16)
        em.wait("tensor", s["tr"])


def emit_dequant(p, em, R, pk_hbm, scT_hbm, out_hbm, n_o, n_w, s,
                 eng="vector", arena=None, q="sync", sems=("ld", "st", "ev")):
    c = R[arena] if arena else R["c"]
    Gtot = n_w * 8 // 128
    ntiles = n_o // 128
    ve = em.eng(eng)
    sy = em.eng(q)
    s_ld, s_st, s_ev = s[sems[0]], s[sems[1]], s[sems[2]]
    WS = 256
    slices = []
    w0 = 0
    while w0 < n_w:
        slices.append((w0, min(WS, n_w - w0)))
        w0 += WS
    pkc = 2 * n_w
    wsb = c[:, 6144:] if arena else R["w"]
    assert (not arena) or (6144 + 2 * pkc + 2 * Gtot <= 32768)
    wa = c[:, 4096:4608].bitcast(I32)
    wb = c[:, 4608:5120].bitcast(I32)
    wc = c[:, 5120:5632].bitcast(I32)
    wd = (c[:, 5632:6144] if arena else R["f"][:, 2048:2304]).bitcast(I32)
    pk = [wsb[:, 0:pkc].bitcast(I32),
          wsb[:, pkc:2 * pkc].bitcast(I32)]
    scb = 2 * pkc
    scf = [wsb[:, scb + i * Gtot:scb + (i + 1) * Gtot]
           .bitcast(mybir.dt.float16) for i in range(2)]
    sc32 = (c[:, 30000:30000 + 2 * Gtot].bitcast(F32) if arena
            else R["f"][:, 0:Gtot])
    ld_tgt = []
    comp_tgt = []
    buf_store = {}
    outbuf_ct = 0
    for i in range(ntiles + 1):
        if i < ntiles:
            if i >= 2:
                em.wait("sync", s_ev, comp_tgt[i - 2])
            em.inc(sy.dma_start(out=pk[i % 2][:],
                                in_=pk_hbm[i * 128:(i + 1) * 128, :]),
                   s_ld, 16)
            t1 = em.inc(sy.dma_start(out=scf[i % 2][:],
                                     in_=scT_hbm[i * 128:(i + 1) * 128, :]),
                        s_ld, 16)
            ld_tgt.append(t1)
        if i == 0:
            continue
        j = i - 1
        b = j % 2
        em.wait(eng, s_ld, ld_tgt[j])
        em.chain(eng, ve.tensor_copy(sc32, scf[b][:]), s_ev)
        for (w0, ws) in slices:
            ob = outbuf_ct % 2
            outbuf_ct += 1
            out_bf = c[:, ob * 2048:ob * 2048 + 8 * ws]
            out_i = out_bf.bitcast(I32)
            if ob in buf_store:
                em.wait(eng, s_st, buf_store[ob])
            pks = pk[b][:, w0:w0 + ws]
            for pp in range(4):
                em.chain(eng, ve.tensor_scalar(wa[:, 0:ws], pks, 8 * pp, 0xFF,
                                               ALU.logical_shift_right,
                                               ALU.bitwise_and), s_ev)
                em.chain(eng, ve.tensor_scalar(wa[:, 0:ws], wa[:, 0:ws], 4097,
                                               None, ALU.mult), s_ev)
                em.chain(eng, ve.tensor_scalar(wb[:, 0:ws], wa[:, 0:ws],
                                               0x80008, 12, ALU.bitwise_and,
                                               ALU.logical_shift_left), s_ev)
                em.chain(eng, ve.tensor_scalar(wc[:, 0:ws], wa[:, 0:ws],
                                               7, None, ALU.bitwise_and),
                         s_ev)
                em.chain(eng, ve.tensor_scalar(wc[:, 0:ws], wc[:, 0:ws],
                                               252, None, ALU.add), s_ev)
                em.chain(eng, ve.tensor_scalar(wc[:, 0:ws], wc[:, 0:ws],
                                               6, None,
                                               ALU.logical_shift_left), s_ev)
                em.chain(eng, ve.tensor_scalar(wa[:, 0:ws], wa[:, 0:ws],
                                               0x70000, None, ALU.bitwise_and),
                         s_ev)
                em.chain(eng, ve.tensor_scalar(wa[:, 0:ws], wa[:, 0:ws],
                                               0xFC0000, None, ALU.add),
                         s_ev)
                em.chain(eng, ve.tensor_scalar(wa[:, 0:ws], wa[:, 0:ws],
                                               6, None,
                                               ALU.logical_shift_left), s_ev)
                em.chain(eng, ve.tensor_tensor(wa[:, 0:ws], wa[:, 0:ws],
                                               wc[:, 0:ws], ALU.bitwise_or),
                         s_ev)
                ubf = wa[:, 0:ws].bitcast(BF16)
                tbf = wd[:, 0:ws].bitcast(BF16)
                em.chain(eng, ve.tensor_scalar(tbf, ubf, 2.0, 1.0, ALU.mult,
                                               ALU.subtract), s_ev)
                em.chain(eng, ve.tensor_tensor(ubf, ubf, tbf, ALU.min), s_ev)
                dst = out_i.rearrange("q (w four) -> q w four", four=4)[:, :, pp]
                em.chain(eng, ve.tensor_tensor(dst, wa[:, 0:ws], wb[:, 0:ws],
                                               ALU.bitwise_or), s_ev)
            g0 = w0 // 16
            ng = ws // 16
            last = None
            for g in range(ng):
                last = ve.tensor_scalar(out_bf[:, g * 128:(g + 1) * 128],
                                        out_bf[:, g * 128:(g + 1) * 128],
                                        sc32[:, g0 + g:g0 + g + 1], None,
                                        ALU.mult)
            em.chain(eng, last, s_ev)
            em.wait("sync", s_ev)
            t2 = em.inc(sy.dma_start(
                out=out_hbm[j * 128:(j + 1) * 128, w0 * 8:w0 * 8 + ws * 8],
                in_=out_bf), s_st, 16)
            buf_store[ob] = t2
        comp_tgt.append(s_ev.n)
    em.wait("sync", s_st)
    em.wait(eng, s_st)


def emit_ffn0(p, em, R, wgu_hbm, wdn_hbm, h_hbm, y_hbm, s):
    a, w, c, cols, f = R["a"], R["w"], R["c"], R["cols"], R["f"]
    t, v, sc, sy = em.tensor, em.vector, em.scalar, em.sync
    I = 5632
    NOC = I // 512
    mm_prev_ocp = 0
    ev_bank = {}
    st_stage = {}
    for ocp in range(NOC):
        em.wait("sync", s["mm"], mm_prev_ocp)
        for half, row0 in ((0, ocp * 512), (1, I + ocp * 512)):
            for dch in range(DC):
                ins = sy.dma_start(
                    out=w[:, half * 8192 + dch * 512:half * 8192 + dch * 512 + 512],
                    in_=wgu_hbm[row0:row0 + 512, dch * 128:(dch + 1) * 128],
                    transpose=True)
                s["wg"].n += 16
                ins.then_inc(s["wg"].h, 16)
        em.wait("tensor", s["wg"])
        for tt in range(TT):
            bg, bu = (tt % 2) * 2, (tt % 2) * 2 + 1
            pg, pu = R[f"p{bg}"], R[f"p{bu}"]
            for b in (bg, bu):
                if b in ev_bank:
                    em.wait("tensor", s["ev"], ev_bank[b])
            for dch in range(DC):
                lhs = a[:, dch * T + tt * 128:dch * T + tt * 128 + 128]
                t.matmul(pg[:], lhs, w[:, dch * 512:dch * 512 + 512],
                         start=(dch == 0), stop=(dch == DC - 1))
                ins = t.matmul(pu[:], lhs, w[:, 8192 + dch * 512:8192 + dch * 512 + 512],
                               start=(dch == 0), stop=(dch == DC - 1))
            mmt = em.inc(ins, s["mm"], 1)
            em.wait("scalar", s["mm"], mmt)
            sg = f[:, 2048 + (tt % 2) * 512:2048 + (tt % 2) * 512 + 512]
            em.inc(sc.activation(sg, pg[:], AF.Silu), s["ev"], 1)
            em.wait("vector", s["ev"])
            stg = (tt % 4)
            stage = c[:, stg * 512:stg * 512 + 512]
            if stg in st_stage:
                em.wait("vector", s["st"], st_stage[stg])
            ins = v.tensor_tensor(stage, sg, pu[:], ALU.mult)
            evt = em.inc(ins, s["ev"], 1)
            ev_bank[bg] = evt
            ev_bank[bu] = evt
            em.wait("sync", s["ev"], evt)
            st_stage[stg] = em.inc(
                sy.dma_start(out=h_hbm[tt * 128:(tt + 1) * 128,
                                       ocp * 512:ocp * 512 + 512], in_=stage),
                s["st"], 16)
        mm_prev_ocp = s["mm"].n
    em.wait("sync", s["st"])
    NICH = I // 128
    ev_bank = {}
    for dhalf in range(2):
        em.wait("sync", s["mm"])
        for ich in range(NICH):
            dst = (a[:, ich * 1024:(ich + 1) * 1024] if ich < 32 else
                   w[:, (ich - 32) * 1024:(ich - 31) * 1024])
            ins = sy.dma_start(
                out=dst,
                in_=wdn_hbm[dhalf * 1024:(dhalf + 1) * 1024,
                            ich * 128:(ich + 1) * 128],
                transpose=True)
            s["wg"].n += 16
            ins.then_inc(s["wg"].h, 16)
        em.wait("tensor", s["wg"])
        for tp in range(8):
            ht = [c[:, 4096 + k * 256:4096 + (k + 1) * 256] for k in range(2)]
            mm_base = s["mm"].n
            for b in range(4):
                if b in ev_bank:
                    em.wait("tensor", s["ev"], ev_bank[b])
            for ich in range(NICH):
                hb = ich % 2
                if ich >= 2:
                    em.wait("sync", s["mm"], mm_base + ich - 1)
                else:
                    em.wait("sync", s["mm"], mm_base)
                ins = sy.dma_start(
                    out=ht[hb],
                    in_=h_hbm[tp * 256:(tp + 1) * 256,
                              ich * 128:(ich + 1) * 128], transpose=True)
                tld = em.inc(ins, s["tr"], 16)
                em.wait("tensor", s["tr"], tld)
                wt_t = (a[:, ich * 1024:(ich + 1) * 1024] if ich < 32 else
                        w[:, (ich - 32) * 1024:(ich - 31) * 1024])
                for ttl in range(2):
                    for dcol in range(2):
                        ins = t.matmul(
                            R[f"p{ttl * 2 + dcol}"][:],
                            ht[hb][:, ttl * 128:(ttl + 1) * 128],
                            wt_t[:, dcol * 512:(dcol + 1) * 512],
                            start=(ich == 0), stop=(ich == NICH - 1))
                ins.then_inc(s["mm"].h, 1)
                s["mm"].n += 1
            mmt = s["mm"].n
            em.wait("scalar", s["mm"], mmt)
            evt = None
            for ttl in range(2):
                tt = tp * 2 + ttl
                for dcol in range(2):
                    b = ttl * 2 + dcol
                    ps = R[f"p{b}"]
                    stage = c[:, b * 512:(b + 1) * 512]
                    em.wait("scalar", s["st"], st_stage.get(("dn", b), 0))
                    em.inc(sc.copy(stage, ps[:]), s["ev"], 1)
                    qcol = cols[:, 256 + tt * 4 + dhalf * 2 + dcol:
                                257 + tt * 4 + dhalf * 2 + dcol]
                    evt = em.inc(sc.activation(f[:, 2048:2560], ps[:], AF.Square,
                                               accum_out=qcol), s["ev"], 1)
                    em.wait("sync", s["ev"], evt)
                    st_stage[("dn", b)] = em.inc(
                        sy.dma_start(out=y_hbm[tt * 128:(tt + 1) * 128,
                                               dhalf * 1024 + dcol * 512:
                                               dhalf * 1024 + dcol * 512 + 512],
                                     in_=stage), s["st"], 16)
                    ev_bank[b] = evt
        dn_prev = s["mm"].n
    em.wait("sync", s["st"])


def emit_rms_res(p, em, R, y_hbm, s, scale=0.5):
    x, c, cols, f = R["x"], R["c"], R["cols"], R["f"]
    v, sc, g = em.vector, em.scalar, em.gpsimd
    ss = cols[:, 448:448 + TT]
    for tt in range(TT):
        src = cols[:, 256 + tt * 4:256 + tt * 4 + 4]
        i = v.tensor_reduce(ss[:, tt:tt + 1], src, mybir.AxisListType.X, ALU.add)
    em.chain("vector", i, s["ev"])
    srt = cols[:, 464:464 + TT]
    rr = cols[:, 480:480 + TT]
    em.inc(v.tensor_scalar(ss, ss, 1.0 / D, None, ALU.mult), s["ev"], 1)
    em.wait("scalar", s["ev"])
    em.inc(sc.activation(srt, ss, AF.Sqrt, bias=R["eps"][:, 1:2]), s["ev"], 1)
    em.wait("vector", s["ev"])
    em.chain("vector", v.reciprocal(rr, srt), s["ev"])
    em.chain("vector", v.tensor_scalar(rr, rr, scale, None, ALU.mult), s["ev"])
    for tt in range(TT):
        yb = tt % 2
        ystage = c[:, yb * 2048:(yb + 1) * 2048]
        ins = g.dma_start(out=ystage, in_=y_hbm[tt * 128:(tt + 1) * 128, :])
        tld = em.inc(ins, s["ld"], 16)
        em.wait("vector", s["ld"], tld)
        xt = x[:, tt * D:(tt + 1) * D]
        ins = v.scalar_tensor_tensor(xt, ystage, rr[:, tt:tt + 1], xt,
                                     ALU.mult, ALU.add)
        evt = em.inc(ins, s["ev"], 1)
        if tt + 2 < TT:
            em.wait("gpsimd", s["ev"], evt)
    em.wait("vector", s["ev"])



def emit_qkv_h(p, em, R, wq_hbm, xn_full, q_hbm, k_hbm, v_hbm, qn_hbm, kn_hbm,
               s, bias_hbm=None):
    a, w, c, cols, f = R["a"], R["w"], R["c"], R["cols"], R["f"]
    t, v, sc, sy, g = em.tensor, em.vector, em.scalar, em.sync, em.gpsimd
    em.wait("gpsimd", s["mm"])
    em.wait("gpsimd", s["ev"])
    for oc in range(8):
        ins = g.dma_start(out=a[:, oc * 2048:(oc + 1) * 2048]
                          .rearrange("q (dch o) -> q dch o", o=128),
                          in_=wq_hbm[:, oc * 128:(oc + 1) * 128]
                          .rearrange("(dch q) o -> q dch o", q=128))
        em.inc(ins, s["wg"], 16)
    ins = g.dma_start(out=a[:, 16384:24576]
                      .rearrange("q (dch o) -> q dch o", o=512),
                      in_=wq_hbm[:, 1024:1536]
                      .rearrange("(dch q) o -> q dch o", q=128))
    em.inc(ins, s["wg"], 16)
    if bias_hbm is not None:
        for oc in range(4):
            ins = g.dma_start(out=f[:, 3776 + oc:3777 + oc],
                              in_=bias_hbm[oc * 128:(oc + 1) * 128]
                              .rearrange("(x y) -> x y", y=1))
            em.inc(ins, s["wg"], 16)
        for oc in range(4):
            ins = g.dma_start(out=f[:, 3780 + oc:3781 + oc],
                              in_=bias_hbm[512 + oc * 128:512 + (oc + 1) * 128]
                              .rearrange("(x y) -> x y", y=1))
            em.inc(ins, s["wg"], 16)
        ins = g.dma_start(out=R["r1"][0:1, 512:1024],
                          in_=bias_hbm[1024:1536].rearrange("(x y) -> x y", x=1))
        em.inc(ins, s["wg"], 16)
        em.wait("vector", s["wg"])
        em.chain("vector", v.tensor_copy(f[0:1, 2560:3072],
                                         R["r1"][0:1, 512:1024]), s["ev"])
        em.wait("tensor", s["ev"])
        ins = t.matmul(R["p3"][:], R["ones1f"][:], f[0:1, 2560:3072],
                       start=True, stop=True)
        bmt = em.inc(ins, s["mm"], 1)
        em.wait("vector", s["mm"], bmt)
        em.chain("vector", v.tensor_copy(f[:, 2560:3072], R["p3"][:]), s["ev"])
    em.wait("sync", s["wg"])
    ev_bank = {}
    mm_tc = {}
    tr_tc = {}

    def load_tc(tc):
        wb = (tc % 2) * 8192
        if tc >= 2:
            em.wait("sync", s["mm"], mm_tc[tc - 2])
        for dch in range(DC):
            ins = sy.dma_start(out=w[:, wb + dch * 512:wb + (dch + 1) * 512],
                               in_=xn_full[tc * 512:(tc + 1) * 512,
                                           dch * 128:(dch + 1) * 128],
                               transpose=True)
            tr_tc[tc] = em.inc(ins, s["tr"], 16)

    load_tc(0)
    for tc in range(16):
        if tc + 1 < 16:
            load_tc(tc + 1)
        wb = (tc % 2) * 8192
        em.wait("tensor", s["tr"], tr_tc[tc])
        for oc in range(8):
            b = oc % 2
            pq = R[f"p{b}"]
            if b in ev_bank:
                em.wait("tensor", s["ev"], ev_bank[b])
            if "pn" in ev_bank:
                em.wait("tensor", s["ev"], ev_bank["pn"])
            for dch in range(DC):
                ins = t.matmul(pq[:], a[:, oc * 2048 + dch * 128:
                                        oc * 2048 + (dch + 1) * 128],
                               w[:, wb + dch * 512:wb + (dch + 1) * 512],
                               start=(dch == 0), stop=(dch == DC - 1))
            mmt = em.inc(ins, s["mm"], 1)
            em.wait("vector", s["mm"], mmt)
            stage = c[:, b * 512:b * 512 + 512]
            if ("st", b) in ev_bank:
                em.wait("vector", s["st"], ev_bank[("st", b)])
            isq = oc < 4
            if bias_hbm is not None:
                bcol = f[:, 3776 + oc:3777 + oc]
                if isq:
                    i2 = v.tensor_scalar(stage, pq[:], bcol, RSD,
                                         ALU.add, ALU.mult)
                else:
                    i2 = v.tensor_scalar(stage, pq[:], bcol, None, ALU.add)
            else:
                if isq:
                    i2 = v.tensor_scalar(stage, pq[:], RSD, None, ALU.mult)
                else:
                    i2 = v.tensor_copy(stage, pq[:])
            em.chain("vector", i2, s["ev"])
            ev_bank[b] = s["ev"].n
            em.wait("scalar", s["ev"], ev_bank[b])
            sq = c[:, 2048 + b * 512:2048 + b * 512 + 512]
            i3 = sc.activation(sq, stage, AF.Square)
            sqt = em.inc(i3, s["ev"], 1)
            em.wait("tensor", s["ev"], sqt)
            pn = R["p2"]
            ins = t.matmul(pn[0:1, :], R["onesc"][:, 0:1], sq,
                           start=True, stop=True)
            nmt = em.inc(ins, s["mm"], 1)
            em.wait("scalar", s["mm"], nmt)
            nrow = f[0:1, 2048:2560]
            if isq:
                i4 = sc.activation(nrow, pn[0:1, :], AF.Sqrt)
            else:
                i4 = sc.copy(nrow, pn[0:1, :])
            qnt = em.inc(i4, s["ev"], 1)
            ev_bank["pn"] = qnt
            em.wait("sync", s["ev"], qnt)
            dstn = qn_hbm if isq else kn_hbm
            em.inc(sy.dma_start(out=dstn[oc % 4:oc % 4 + 1,
                                         tc * 512:(tc + 1) * 512],
                                in_=nrow), s["st"], 16)
            dst = q_hbm if isq else k_hbm
            ev_bank[("st", b)] = em.inc(
                sy.dma_start(out=dst[(oc % 4) * 128:(oc % 4) * 128 + 128,
                                     tc * 512:(tc + 1) * 512],
                             in_=stage), s["st"], 16)
        for tt4 in range(4):
            b = tt4 % 2
            pq = R[f"p{4 + b}"]
            if ("v", b) in ev_bank:
                em.wait("tensor", s["ev"], ev_bank[("v", b)])
            for dch in range(DC):
                ins = t.matmul(pq[:],
                               w[:, wb + dch * 512 + tt4 * 128:
                                 wb + dch * 512 + tt4 * 128 + 128],
                               a[:, 16384 + dch * 512:16384 + (dch + 1) * 512],
                               start=(dch == 0), stop=(dch == DC - 1))
            mmt = em.inc(ins, s["mm"], 1)
            em.wait("vector", s["mm"], mmt)
            stage = c[:, 4096 + b * 512:4096 + b * 512 + 512]
            if ("vst", b) in ev_bank:
                em.wait("vector", s["st"], ev_bank[("vst", b)])
            if bias_hbm is not None:
                i2 = v.tensor_tensor(stage, pq[:], f[:, 2560:3072], ALU.add)
            else:
                i2 = v.tensor_copy(stage, pq[:])
            em.chain("vector", i2, s["ev"])
            ev_bank[("v", b)] = s["ev"].n
            em.wait("sync", s["ev"], ev_bank[("v", b)])
            ev_bank[("vst", b)] = em.inc(
                sy.dma_start(out=v_hbm[tc * 512 + tt4 * 128:
                                       tc * 512 + (tt4 + 1) * 128, :],
                             in_=stage), s["st"], 16)
        mm_tc[tc] = s["mm"].n
    em.wait("sync", s["st"])


def emit_attn_masks(p, em, R, s):
    g = em.gpsimd
    w = R["w"]
    em.wait("gpsimd", s["mm"])
    io16 = w[:, 10240:10752].bitcast(mybir.dt.int16)
    g.iota(io16, pattern=[[1, 512]], base=0, channel_multiplier=-1,
           allow_small_or_imprecise_dtypes=True)
    ve = em.vector
    em.wait("vector", s["mm"])
    em.inc(g.engine_nop(), s["init"], 1)
    em.wait("vector", s["init"])
    for m in range(4):
        sl = w[:, 8192 + m * 512:8192 + (m + 1) * 512].bitcast(mybir.dt.int16)
        em.chain("vector", ve.tensor_scalar(sl, io16, 128 * m, None, ALU.is_lt),
                 s["ev"])
    g.iota(R["f"][:, 3520:3584], pattern=[[128, 64]], base=0,
           channel_multiplier=1, allow_small_or_imprecise_dtypes=True)
    ins = g.iota(R["r1"][0:1, 0:512], pattern=[[1, 512]], base=0,
                 channel_multiplier=0, allow_small_or_imprecise_dtypes=True)
    em.inc(ins, s["init"], 1)
    for e in ("tensor", "vector", "scalar", "sync"):
        em.wait(e, s["init"])


def emit_attn_flash_h(p, em, R, q_hbm, k_hbm, v_hbm, qn_hbm, kn_hbm, attn_cc,
                      s, alibi, slopes_hbm=None):
    a, w, c, cols, f = R["a"], R["w"], R["c"], R["cols"], R["f"]
    t, v, sc, sy, g = em.tensor, em.vector, em.scalar, em.sync, em.gpsimd
    r1, r2 = R["r1"], R["r2"]
    if alibi:
        ins = sy.dma_start(out=f[0:1, 3256:3260],
                           in_=slopes_hbm.rearrange("(x y) -> x y", x=1))
        em.inc(ins, s["tr"], 16)
        em.wait("tensor", s["tr"])
        ins = em.tensor.matmul(R["p6"][:, 0:4], R["ones1f"][:],
                               f[0:1, 3256:3260], start=True, stop=True)
        bmt = em.inc(ins, s["mm"], 1)
        em.wait("vector", s["mm"], bmt)
        em.chain("vector", v.tensor_copy(f[:, 3800:3804], R["p6"][:, 0:4]),
                 s["ev"])
        em.wait("tensor", s["ev"])
    if True:
        for hh in range(4):
            mx = f[0:1, 3230 + hh:3231 + hh]
            for piece in range(4):
                ins = sy.dma_start(out=f[0:1, 1024:1024 + 2048],
                                   in_=kn_hbm[hh:hh + 1,
                                              piece * 2048:(piece + 1) * 2048])
                tt_ = em.inc(ins, s["tr"], 16)
                em.wait("vector", s["tr"], tt_)
                if piece == 0:
                    em.chain("vector",
                             v.tensor_reduce(mx, f[0:1, 1024:3072],
                                             mybir.AxisListType.X, ALU.max),
                             s["ev"])
                else:
                    pm = f[0:1, 3250:3251]
                    em.chain("vector",
                             v.tensor_reduce(pm, f[0:1, 1024:3072],
                                             mybir.AxisListType.X, ALU.max),
                             s["ev"])
                    em.chain("vector", v.tensor_tensor(mx, mx, pm, ALU.max),
                             s["ev"])
                em.wait("sync", s["ev"])
        em.wait("scalar", s["ev"])
        last = None
        for hh in range(4):
            last = sc.activation(f[0:1, 3240 + hh:3241 + hh],
                                 f[0:1, 3230 + hh:3231 + hh], AF.Sqrt)
        em.inc(last, s["ev"], 1)
        em.wait("vector", s["ev"])

    mm_kbuf = {}
    ev_outbank = {}
    ev_pt = {}
    st_stage = {}
    for hh in range(4):
        for qt in range(16):
            g0 = qt * 512
            nksc = g0 // 128 + 4
            cmax = (g0 + 512 + 1023) // 1024
            ins = sy.dma_start(out=c[:, 6144:6656],
                               in_=q_hbm[hh * 128:(hh + 1) * 128,
                                         g0:g0 + 512])
            qtt = em.inc(ins, s["tr"], 16)
            ins = sy.dma_start(out=f[0:1, 1024:1536],
                               in_=qn_hbm[hh:hh + 1, g0:g0 + 512])
            em.inc(ins, s["tr"], 16)
            em.wait("vector", s["tr"])
            qnrow = f[0:1, 1024:1536]
            em.chain("vector",
                     v.tensor_scalar(qnrow, qnrow, f[0:1, 3240 + hh:3241 + hh],
                                     None, ALU.mult), s["ev"])
            if alibi:
                em.chain("vector",
                         v.scalar_tensor_tensor(qnrow, r1[0:1, 0:512],
                                                f[0:1, 3256 + hh:3257 + hh],
                                                qnrow, ALU.mult, ALU.add),
                         s["ev"])
            ins = v.tensor_scalar(r2[0:1, 0:512], qnrow, -1.0, None, ALU.mult)
            em.chain("vector", ins, s["ev"])
            if alibi:
                ins = v.tensor_scalar(f[:, 3584:3648], f[:, 3520:3584],
                                      1.0, -float(g0), ALU.mult, ALU.add)
                em.chain("vector", ins, s["ev"])
                ins = v.tensor_scalar(cols[:, 320:384], f[:, 3584:3648],
                                      f[:, 3800 + hh:3801 + hh], None, ALU.mult)
                em.chain("vector", ins, s["ev"])
            seeds_t = s["ev"].n
            em.wait("tensor", s["ev"], seeds_t)
            em.wait("scalar", s["ev"], seeds_t)
            em.wait("tensor", s["tr"], qtt)
            if 0 in ev_outbank:
                em.wait("tensor", s["ev"], ev_outbank[0])
            done = 0
            nchunks = cmax
            kbs = list(range(0, g0 + 512, 128))

            def issue_chunk_loads(cg):
                par = cg % 2
                if par in mm_kbuf:
                    em.wait("sync", s["mm"], mm_kbuf[par])
                    em.wait("gpsimd", s["mm"], mm_kbuf[par])
                klen = min(1024, g0 + 512 - cg * 1024)
                ins = sy.dma_start(out=c[:, par * 1024:par * 1024 + klen],
                                   in_=k_hbm[hh * 128:(hh + 1) * 128,
                                             cg * 1024:cg * 1024 + klen])
                ktt = em.inc(ins, s["tr"], 16)
                ins = g.dma_start(out=c[:, 2048 + par * 1024:
                                        2048 + par * 1024 + klen]
                                  .rearrange("q (sk d) -> q sk d", d=128),
                                  in_=v_hbm[cg * 1024:cg * 1024 + klen,
                                            hh * 128:(hh + 1) * 128]
                                  .rearrange("(sk q) d -> q sk d", q=128))
                vtt = em.inc(ins, s["ld"], 16)
                return ktt, vtt

            loadtgt = {}
            loadtgt[0] = issue_chunk_loads(0)
            if nchunks > 1:
                loadtgt[1] = issue_chunk_loads(1)

            def seed_qk(i):
                kb = kbs[i]
                cg = kb // 1024
                par = cg % 2
                ksc = (kb % 1024) // 128
                sb = 4 + i % 2
                Sp = R[f"p{sb}"]
                if ("exp", i % 2) in ev_pt:
                    em.wait("tensor", s["ev"], ev_pt[("exp", i % 2)])
                if i == 0 or kbs[i - 1] // 1024 != cg:
                    em.wait("tensor", s["tr"], loadtgt[cg][0])
                    em.wait("tensor", s["ld"], loadtgt[cg][1])
                t.matmul(Sp[:], R["ones1"][:], r2[0:1, 0:512],
                         start=True, stop=False)
                ins = t.matmul(Sp[:], c[:, par * 1024 + ksc * 128:
                                        par * 1024 + ksc * 128 + 128],
                               c[:, 6144:6656], start=False, stop=True)
                mmt = em.inc(ins, s["mm"], 1)
                em.wait("scalar", s["mm"], mmt)
                if ("pt", i % 2) in ev_pt:
                    em.wait("scalar", s["mm"], ev_pt[("pt", i % 2)])
                PT = c[:, 4096 + (i % 2) * 512:4096 + (i % 2) * 512 + 512]
                bias = (cols[:, 320 + kb // 128:321 + kb // 128]
                        if alibi else 0.0)
                ins = sc.activation(PT, Sp[:], AF.Exp, bias=bias)
                expt = em.inc(ins, s["ev"], 1)
                ev_pt[("exp", i % 2)] = expt
                gate = expt
                if kb >= g0:
                    m = (kb - g0) // 128
                    em.wait("vector", s["ev"], expt)
                    msk = w[:, 8192 + m * 512:8192 + (m + 1) * 512] \
                        .bitcast(mybir.dt.int16)
                    ins = v.copy_predicated(PT, msk, R["zero"][:])
                    gate = em.inc(ins, s["ev"], 1)
                ev_pt[("gate", i % 2)] = gate

            def sum_av(i):
                kb = kbs[i]
                cg = kb // 1024
                par = cg % 2
                ksc = (kb % 1024) // 128
                PT = c[:, 4096 + (i % 2) * 512:4096 + (i % 2) * 512 + 512]
                em.wait("tensor", s["ev"], ev_pt[("gate", i % 2)])
                first = i == 0
                last = i == nksc - 1
                t.matmul(R["p2"][0:1, :], R["onesc"][:, 0:1], PT,
                         start=first, stop=last)
                ins = t.matmul(R["p0"][:],
                               c[:, 2048 + par * 1024 + ksc * 128:
                                 2048 + par * 1024 + ksc * 128 + 128],
                               PT, start=first, stop=last)
                ev_pt[("pt", i % 2)] = em.inc(ins, s["mm"], 1)
                if kb % 1024 == 1024 - 128 or i == nksc - 1:
                    mm_kbuf[par] = s["mm"].n
                    if cg + 2 < nchunks and (cg + 2) not in loadtgt:
                        loadtgt[cg + 2] = issue_chunk_loads(cg + 2)

            seed_qk(0)
            for i in range(nksc):
                if i + 1 < nksc:
                    seed_qk(i + 1)
                sum_av(i)
            em.wait("vector", s["mm"])
            drow = f[0:1, 512:1024]
            em.chain("vector", v.tensor_copy(drow, R["p2"][0:1, :]), s["ev"])
            em.chain("vector", v.reciprocal(drow, drow), s["ev"])
            em.wait("tensor", s["ev"])
            ins = t.matmul(R["p6"][:], R["ones1f"][:], drow,
                           start=True, stop=True)
            bmt = em.inc(ins, s["mm"], 1)
            em.wait("vector", s["mm"], bmt)
            em.chain("vector", v.tensor_copy(f[:, 0:512], R["p6"][:]), s["ev"])
            stage = c[:, 7168:7680]
            if "st" in st_stage:
                em.wait("vector", s["st"], st_stage["st"])
            ins = v.tensor_tensor(stage, R["p0"][:], f[:, 0:512], ALU.mult)
            ev_outbank[0] = em.inc(ins, s["ev"], 1)
            em.wait("sync", s["ev"], ev_outbank[0])
            st_stage["st"] = em.inc(
                sy.dma_start(out=attn_cc[hh * 128:(hh + 1) * 128,
                                         qt * 512:(qt + 1) * 512], in_=stage),
                s["st"], 16)
    em.wait("sync", s["st"])
    em.wait("tensor", s["st"])


RG = [[0, 1, 2, 3], [4, 5, 6, 7]]


def emit_coll(p, em, R, kind, cc_in, cc_out, s, gap_fn=None):
    g = em.gpsimd
    em.wait("gpsimd", s["st"])
    em.wait("gpsimd", s["out"])
    ins = g.collective_compute(kind, ALU.bypass, replica_groups=RG,
                               ins=[cc_in.opt()], outs=[cc_out.opt()])
    em.inc(ins, s["cc"], 1)
    if gap_fn is not None:
        gap_fn()
    for e in ENGINES:
        em.wait(e, s["cc"])


def emit_attn_load_a(p, em, R, attn_full, tok0_hbm, s):
    import bass_rust
    g, sy = em.gpsimd, em.sync
    a = R["a"]
    em.inc(g.dma_start(out=R["ioff"][0:1, 0:1], in_=tok0_hbm), s["ld"], 16)
    em.wait("sync", s["ld"])
    em.wait("sync", s["mm"])
    em.wait("sync", s["ev"])
    if em.active == "sync":
        if "_tokval" not in R:
            reg = sy.alloc_register("tok0reg")
            sy.reg_load(reg, R["ioff"][0:1, 0:1])
            R["_tokval"] = bass_rust.RuntimeValue(reg, min_val=0,
                                                  max_val=SEQ - T)
        val = R["_tokval"]
    else:
        val = 0
    last = None
    for ach in range(DC):
        last = sy.dma_start(out=a[:, ach * T:(ach + 1) * T],
                            in_=attn_full[ach * 128:(ach + 1) * 128,
                                          bass.ds(val, T)])
        em.inc(last, s["ld"], 16)
    em.wait("tensor", s["ld"])
    em.wait("gpsimd", s["ld"])


def emit_outproj_res(p, em, R, wo_hbm, s, bo_hbm=None):
    a, w, c, cols, f = R["a"], R["w"], R["c"], R["cols"], R["f"]
    t, v, sc, sy, g = em.tensor, em.vector, em.scalar, em.sync, em.gpsimd
    ev_bank = {}
    em.wait("gpsimd", s["ev"])
    for oc4 in range(4):
        em.wait("gpsimd", s["mm"])
        ins = g.dma_start(out=w[:, 0:8192]
                          .rearrange("q (ach o) -> q ach o", o=512),
                          in_=wo_hbm[:, oc4 * 512:(oc4 + 1) * 512]
                          .rearrange("(ach q) o -> q ach o", q=128))
        em.inc(ins, s["wg"], 16)
        if bo_hbm is not None:
            ins = g.dma_start(out=R["r1"][0:1, 512:1024],
                              in_=bo_hbm[oc4 * 512:(oc4 + 1) * 512]
                              .rearrange("(x y) -> x y", x=1))
            em.inc(ins, s["wg"], 16)
            em.wait("vector", s["wg"])
            em.chain("vector", v.tensor_copy(f[0:1, 2560:3072],
                                             R["r1"][0:1, 512:1024]), s["ev"])
            em.wait("tensor", s["ev"])
            if "p3" in ev_bank:
                em.wait("tensor", s["ev"], ev_bank["p3"])
            ins = t.matmul(R["p3"][:], R["ones1f"][:], f[0:1, 2560:3072],
                           start=True, stop=True)
            bmt = em.inc(ins, s["mm"], 1)
            em.wait("vector", s["mm"], bmt)
            em.chain("vector", v.tensor_copy(f[:, 2560:3072], R["p3"][:]),
                     s["ev"])
            ev_bank["p3"] = s["ev"].n
        em.wait("tensor", s["wg"])
        for tt in range(TT):
            b = tt % 2
            pq = R[f"p{b}"]
            if b in ev_bank:
                em.wait("tensor", s["ev"], ev_bank[b])
            for ach in range(DC):
                ins = t.matmul(pq[:],
                               a[:, ach * T + tt * 128:ach * T + tt * 128 + 128],
                               w[:, ach * 512:(ach + 1) * 512],
                               start=(ach == 0), stop=(ach == DC - 1))
            mmt = em.inc(ins, s["mm"], 1)
            em.wait("vector", s["mm"], mmt)
            xt = R["x"][:, tt * D + oc4 * 512:tt * D + (oc4 + 1) * 512]
            if bo_hbm is not None:
                em.chain("vector", v.tensor_tensor(xt, xt, f[:, 2560:3072],
                                                   ALU.add), s["ev"])
            ins = v.tensor_tensor(xt, xt, pq[:], ALU.add)
            ev_bank[b] = em.inc(ins, s["ev"], 1)
    em.wait("vector", s["ev"])


MAGIC = 1.5 * 2 ** 23


def emit_ternarize(p, em, R, w_hbm, t_hbm, n_r, n_c, s, eng="vector"):
    c, cols, f = R["c"], R["cols"], R["f"]
    ve = em.eng(eng)
    sc, sy, t, v = em.scalar, em.sync, em.tensor, em.vector
    ntr = n_r // 128
    ncc = n_c // 2048
    ntile = ntr * ncc
    ld_tgt = []
    for i in range(ntile + 1):
        if i < ntile:
            if i >= 2:
                em.wait("sync", s["ev"])
            r0, c0 = (i // ncc) * 128, (i % ncc) * 2048
            ins = sy.dma_start(out=f[:, (i % 2) * 2048:(i % 2) * 2048 + 2048],
                               in_=w_hbm[r0:r0 + 128, c0:c0 + 2048])
            ld_tgt.append(em.inc(ins, s["ld"], 16))
        if i == 0:
            continue
        j = i - 1
        em.wait("scalar", s["ld"], ld_tgt[j])
        ins = sc.activation(c[:, 0:4096].bitcast(F32)[:, 0:2048],
                            f[:, (j % 2) * 2048:(j % 2) * 2048 + 2048],
                            AF.Abs, accum_out=cols[:, 64 + j % 32:65 + j % 32])
        em.inc(ins, s["ev"], 1)
        if (j + 1) % 32 == 0 or j == ntile - 1:
            em.wait("vector", s["ev"])
            nacc = (j % 32) + 1
            if j < 32:
                em.chain("vector",
                         v.tensor_reduce(cols[:, 0:1], cols[:, 64:64 + nacc],
                                         mybir.AxisListType.X, ALU.add), s["ev"])
            else:
                em.chain("vector",
                         v.tensor_reduce(cols[:, 1:2], cols[:, 64:64 + nacc],
                                         mybir.AxisListType.X, ALU.add), s["ev"])
                em.chain("vector",
                         v.tensor_tensor(cols[:, 0:1], cols[:, 0:1],
                                         cols[:, 1:2], ALU.add), s["ev"])
            em.wait("scalar", s["ev"])
    em.wait("tensor", s["ev"])
    ins = t.matmul(R["p7"][0:1, 0:1], R["onescf"][:, 0:1], cols[:, 0:1],
                   start=True, stop=True)
    em.inc(ins, s["mm"], 1)
    em.wait("vector", s["mm"])
    gam = cols[0:1, 8:9]
    em.chain("vector", v.tensor_scalar(gam, R["p7"][0:1, 0:1],
                                       1.0 / (n_r * n_c), None, ALU.mult),
             s["ev"])
    em.chain("vector", v.tensor_scalar(cols[0:1, 10:11], gam, EPS2, None,
                                       ALU.add), s["ev"])
    em.chain("vector", v.reciprocal(cols[0:1, 9:10], cols[0:1, 10:11]), s["ev"])
    em.wait("tensor", s["ev"])
    ins = t.matmul(R["p7"][:, 4:6], R["ones1f"][:], cols[0:1, 8:10],
                   start=True, stop=True)
    em.inc(ins, s["mm"], 1)
    em.wait("vector", s["mm"])
    gcol = cols[:, 12:13]
    rcol = cols[:, 13:14]
    em.chain("vector", v.tensor_copy(gcol, R["p7"][:, 4:5]), s["ev"])
    em.chain("vector", v.tensor_copy(rcol, R["p7"][:, 5:6]), s["ev"])
    em.wait(eng, s["ev"])
    em.wait("sync", s["ev"])
    ld_tgt = []
    st_tgt = {}
    for i in range(ntile + 1):
        if i < ntile:
            if i >= 2:
                em.wait("sync", s["ev"])
            r0, c0 = (i // ncc) * 128, (i % ncc) * 2048
            ins = sy.dma_start(out=f[:, (i % 2) * 2048:(i % 2) * 2048 + 2048],
                               in_=w_hbm[r0:r0 + 128, c0:c0 + 2048])
            ld_tgt.append(em.inc(ins, s["ld"], 16))
        if i == 0:
            continue
        j = i - 1
        b = j % 2
        em.wait(eng, s["ld"], ld_tgt[j])
        if b in st_tgt:
            em.wait(eng, s["st"], st_tgt[b])
        wt = f[:, b * 2048:b * 2048 + 2048]
        stg = c[:, b * 2048:(b + 1) * 2048]
        em.chain(eng, ve.tensor_scalar(wt, wt, rcol, MAGIC, ALU.mult, ALU.add),
                 s["ev"])
        em.chain(eng, ve.tensor_scalar(wt, wt, MAGIC, 1.0, ALU.subtract,
                                       ALU.min), s["ev"])
        em.chain(eng, ve.tensor_scalar(stg, wt, -1.0, gcol, ALU.max, ALU.mult),
                 s["ev"])
        em.wait("sync", s["ev"])
        r0, c0 = (j // ncc) * 128, (j % ncc) * 2048
        st_tgt[b] = em.inc(sy.dma_start(out=t_hbm[r0:r0 + 128, c0:c0 + 2048],
                                        in_=stg), s["st"], 16)
    em.wait("sync", s["st"])
    em.wait(eng, s["st"])


def emit_ffn1(p, em, R, tfc_hbm, tproj_hbm, h_hbm, h2_hbm, s):
    a, w, c, cols, f = R["a"], R["w"], R["c"], R["cols"], R["f"]
    t, v, sc, sy, g = em.tensor, em.vector, em.scalar, em.sync, em.gpsimd
    NOC = 16
    ev_bank = {}
    st_stage = {}
    for oc in range(NOC):
        em.wait("sync", s["mm"])
        ins = sy.dma_start(out=w[:, 0:8192].rearrange("q (dch o) -> q dch o",
                                                      o=512),
                           in_=tfc_hbm[:, oc * 512:(oc + 1) * 512]
                           .rearrange("(dch q) o -> q dch o", q=128))
        em.inc(ins, s["wg"], 16)
        em.wait("tensor", s["wg"])
        for tt in range(TT):
            b = tt % 2
            pq = R[f"p{b}"]
            if b in ev_bank:
                em.wait("tensor", s["ev"], ev_bank[b])
            for dch in range(DC):
                ins = t.matmul(pq[:],
                               a[:, dch * T + tt * 128:dch * T + tt * 128 + 128],
                               w[:, dch * 512:(dch + 1) * 512],
                               start=(dch == 0), stop=(dch == DC - 1))
            mmt = em.inc(ins, s["mm"], 1)
            em.wait("scalar", s["mm"], mmt)
            stage = c[:, b * 512:b * 512 + 512]
            if b in st_stage:
                em.wait("scalar", s["st"], st_stage[b])
            i2 = sc.activation(stage, pq[:], AF.Gelu,
                               accum_out=cols[:, tt * 16 + oc:tt * 16 + oc + 1])
            evt = em.inc(i2, s["ev"], 1)
            em.wait("vector", s["ev"], evt)
            i3 = v.scalar_tensor_tensor(
                c[:, 2048 + b * 512:2048 + b * 512 + 512], stage, 1.0, stage,
                ALU.mult, ALU.mult,
                accum_out=cols[:, 256 + tt * 16 + oc:257 + tt * 16 + oc])
            evt = em.inc(i3, s["ev"], 1)
            ev_bank[b] = evt
            em.wait("sync", s["ev"], evt)
            st_stage[b] = em.inc(
                sy.dma_start(out=h_hbm[tt * 128:(tt + 1) * 128,
                                       oc * 512:(oc + 1) * 512], in_=stage),
                s["st"], 16)
    em.wait("sync", s["st"])
    em.wait("vector", s["st"])
    n8 = 1.0 / 8192.0
    for tt in range(TT):
        cb = 16 * tt
        em.wait("sync", s["st"])
        ins = sy.dma_start(out=c[:, 0:8192], in_=h_hbm[tt * 128:(tt + 1) * 128, :])
        tld = em.inc(ins, s["ld"], 16)
        em.wait("vector", s["ld"], tld)
        sm = cols[:, 0:1]
        sq = cols[:, 1:2]
        em.chain("vector", v.tensor_reduce(sm, cols[:, cb:cb + 16],
                                           mybir.AxisListType.X, ALU.add),
                 s["ev"])
        em.chain("vector", v.tensor_reduce(sq, cols[:, 256 + cb:256 + cb + 16],
                                           mybir.AxisListType.X, ALU.add),
                 s["ev"])
        nm = cols[:, 2:3]
        nm2 = cols[:, 3:4]
        var = cols[:, 4:5]
        srt = cols[:, 5:6]
        rstd = cols[:, 6:7]
        em.chain("vector", v.tensor_scalar(nm, sm, -n8, None, ALU.mult), s["ev"])
        em.chain("vector", v.tensor_tensor(nm2, nm, nm, ALU.mult), s["ev"])
        em.inc(v.scalar_tensor_tensor(var, sq, n8, nm2, ALU.mult, ALU.subtract),
               s["ev"], 1)
        em.wait("scalar", s["ev"])
        em.inc(sc.activation(srt, var, AF.Sqrt, bias=R["eps"][:, 1:2]), s["ev"], 1)
        em.wait("vector", s["ev"])
        em.chain("vector", v.reciprocal(rstd, srt), s["ev"])
        hmx = cols[:, 7:8]
        hmn = cols[:, 8:9]
        t1 = cols[:, 9:10]
        t2 = cols[:, 10:11]
        gam = cols[:, 11:12]
        gneg = cols[:, 12:13]
        em.chain("vector", v.tensor_reduce(hmx, c[:, 0:8192],
                                           mybir.AxisListType.X, ALU.max),
                 s["ev"])
        em.chain("vector", v.tensor_reduce(hmn, c[:, 0:8192],
                                           mybir.AxisListType.X, ALU.min),
                 s["ev"])
        em.chain("vector", v.tensor_tensor(t1, hmx, nm, ALU.add), s["ev"])
        em.chain("vector", v.tensor_tensor(t2, hmn, nm, ALU.add), s["ev"])
        em.chain("vector", v.tensor_scalar(t2, t2, -1.0, None, ALU.mult), s["ev"])
        em.chain("vector", v.tensor_tensor(gam, t1, t2, ALU.max), s["ev"])
        em.chain("vector", v.tensor_tensor(gam, gam, rstd, ALU.mult), s["ev"])
        em.chain("vector", v.tensor_scalar(gam, gam, CLIPF, None, ALU.mult),
                 s["ev"])
        em.chain("vector", v.tensor_scalar(gneg, gam, -1.0, None, ALU.mult),
                 s["ev"])
        for half in range(2):
            hs = c[:, half * 4096:(half + 1) * 4096]
            tmp = f[:, 0:4096]
            em.chain("vector", v.tensor_scalar(tmp, hs, nm, rstd, ALU.add,
                                               ALU.mult), s["ev"])
            em.chain("vector", v.tensor_scalar(hs, tmp, gam, gneg, ALU.min,
                                               ALU.max), s["ev"])
            em.wait("sync", s["ev"])
            em.inc(sy.dma_start(out=h2_hbm[tt * 128:(tt + 1) * 128,
                                           half * 4096:(half + 1) * 4096],
                                in_=hs), s["st"], 16)
    em.wait("sync", s["st"])
    NICH = 64
    for dq in range(4):
        em.wait("gpsimd", s["mm"])
        em.wait("gpsimd", s["ev"])
        ins = g.dma_start(out=a[:, 0:32768].rearrange("q (ich o) -> q ich o",
                                                      o=512),
                          in_=tproj_hbm[:, dq * 512:(dq + 1) * 512]
                          .rearrange("(ich q) o -> q ich o", q=128))
        em.inc(ins, s["wg"], 16)
        em.wait("tensor", s["wg"])
        for tg in range(2):
            mm_base = s["mm"].n
            em.wait("tensor", s["ev"])
            for ib in range(16):
                hb = ib % 2
                if ib >= 2:
                    em.wait("sync", s["mm"], mm_base + ib - 1)
                else:
                    em.wait("sync", s["mm"], mm_base)
                for k in range(4):
                    ich = ib * 4 + k
                    ins = sy.dma_start(
                        out=c[:, hb * 4096 + k * 1024:hb * 4096 + (k + 1) * 1024],
                        in_=h2_hbm[tg * 1024:(tg + 1) * 1024,
                                   ich * 128:(ich + 1) * 128], transpose=True)
                    em.inc(ins, s["tr"], 16)
                em.wait("tensor", s["tr"])
                for k in range(4):
                    ich = ib * 4 + k
                    for ttl in range(8):
                        ins = t.matmul(
                            R[f"p{ttl}"][:],
                            c[:, hb * 4096 + k * 1024 + ttl * 128:
                              hb * 4096 + k * 1024 + ttl * 128 + 128],
                            a[:, ich * 512:(ich + 1) * 512],
                            start=(ich == 0), stop=(ich == NICH - 1))
                ins.then_inc(s["mm"].h, 1)
                s["mm"].n += 1
            mmt = s["mm"].n
            em.wait("vector", s["mm"], mmt)
            for ttl in range(8):
                tt = tg * 8 + ttl
                xt = R["x"][:, tt * D + dq * 512:tt * D + (dq + 1) * 512]
                ins = v.scalar_tensor_tensor(xt, R[f"p{ttl}"][:], 0.5, xt,
                                             ALU.mult, ALU.add)
                em.inc(ins, s["ev"], 1)
    em.wait("vector", s["ev"])



MODE = "fused"

F16 = mybir.dt.float16
I = 5632


def shard_inputs(inp):
    x = np.asarray(inp["L0_attn_hidden_states"])
    l0_wq = np.asarray(inp["L0_attn_w_qkv"])
    l0_wo = np.asarray(inp["L0_attn_w_o"])
    gu_pk = np.asarray(inp["L0_ffn_gate_up_packed"])
    gu_sc = np.asarray(inp["L0_ffn_gate_up_scales"])
    dn_pk = np.asarray(inp["L0_ffn_down_packed"])
    dn_sc = np.asarray(inp["L0_ffn_down_scales"])
    l1_wq = np.asarray(inp["L1_attn_Wqkv"])
    l1_bq = np.asarray(inp["L1_attn_bqkv"])
    l1_wo = np.asarray(inp["L1_attn_Wo"])
    l1_bo = np.asarray(inp["L1_attn_bo"])
    wfc = np.asarray(inp["L1_ffn_w_fc"])
    wproj = np.asarray(inp["L1_ffn_w_proj"])

    C = np.ascontiguousarray
    l0_wqT = C(l0_wq.T)
    l0_woT = C(l0_wo.T)
    wfcT = C(wfc.T)
    wprojT = C(wproj.T)
    scT_gu = C(gu_sc.T)
    scT_dn = C(dn_sc.T)
    slopes_all = np.array([0.5 ** (k / 2) for k in range(NH)], np.float32)

    def qkv_slice(wT, r):
        return C(np.concatenate([wT[:, r * 512:(r + 1) * 512],
                                 wT[:, D + r * 512:D + (r + 1) * 512],
                                 wT[:, 2 * D + r * 512:2 * D + (r + 1) * 512]],
                                axis=1))

    def b_slice(b, r):
        return C(np.concatenate([b[r * 512:(r + 1) * 512],
                                 b[D + r * 512:D + (r + 1) * 512],
                                 b[2 * D + r * 512:2 * D + (r + 1) * 512]]))

    per_core = []
    for core in range(8):
        b, r = core // 4, core % 4
        per_core.append({
            "x_in": C(x[b, r * 2048:(r + 1) * 2048]),
            "pk_gu": gu_pk, "scT_gu": scT_gu, "pk_dn": dn_pk, "scT_dn": scT_dn,
            "l0_wq": qkv_slice(l0_wqT, r), "l0_wo": l0_woT,
            "l1_wq": qkv_slice(l1_wq, r), "l1_bq": b_slice(l1_bq, r),
            "l1_wo": C(l1_wo), "l1_bo": C(l1_bo),
            "wfcT": wfcT, "wprojT": wprojT,
            "slopes": C(slopes_all[r * 4:(r + 1) * 4]),
            "tok0": np.array([[r * 2048]], np.int32),
        })
    return per_core


def emit_attn_subl(p, em, R, s, dr, l0, gap1=None, gap2=None):
    emit_ln_to_aT(p, em, R, dr["xp"], s, mode="ln", to_aT=False)
    barrier(em, s["bar"])
    emit_coll(p, em, R, "AllGather", dr["xp"], dr["xn_full"], s, gap_fn=gap1)
    if l0:
        emit_qkv_h(p, em, R, dr["l0_wq"], dr["xn_full"], dr["q_hbm"],
                   dr["k_hbm"], dr["v_hbm"], dr["qn_hbm"], dr["kn_hbm"], s)
    else:
        emit_qkv_h(p, em, R, dr["l1_wq"], dr["xn_full"], dr["q_hbm"],
                   dr["k_hbm"], dr["v_hbm"], dr["qn_hbm"], dr["kn_hbm"], s,
                   bias_hbm=dr["l1_bq"])
    barrier(em, s["bar"])
    emit_attn_masks(p, em, R, s)
    barrier(em, s["bar"])
    emit_attn_flash_h(p, em, R, dr["q_hbm"], dr["k_hbm"], dr["v_hbm"],
                      dr["qn_hbm"], dr["kn_hbm"], dr["attn_cc"], s,
                      not l0, slopes_hbm=None if l0 else dr["slopes"])
    barrier(em, s["bar"])
    emit_coll(p, em, R, "AllGather", dr["attn_cc"], dr["attn_full"], s,
              gap_fn=gap2)
    emit_attn_load_a(p, em, R, dr["attn_full"], dr["tok0"], s)
    barrier(em, s["bar"])
    if l0:
        emit_outproj_res(p, em, R, dr["l0_wo"], s)
    else:
        emit_outproj_res(p, em, R, dr["l1_wo"], s, bo_hbm=dr["l1_bo"])


def emit_ffn0_subl(p, em, R, s, dr):
    emit_ln_to_aT(p, em, R, dr["xp"], s, mode="ln")
    barrier(em, s["bar"])
    emit_ffn0(p, em, R, dr["wgu_dq"], dr["wdn_dq"], dr["h0_hbm"],
              dr["y_hbm"], s)
    barrier(em, s["bar"])
    emit_rms_res(p, em, R, dr["y_hbm"], s, scale=0.5)


def emit_ffn1_subl(p, em, R, s, dr):
    emit_ln_to_aT(p, em, R, dr["xp"], s, mode="bitlin")
    barrier(em, s["bar"])
    emit_ffn1(p, em, R, dr["tfc"], dr["tproj"], dr["h1_hbm"], dr["h2_hbm"], s)


def decl_common_scratch(p, dr):
    dr["xp"] = p.dram("xp", [T, D], BF16)
    dr["xn_full"] = p.dram("xn_full", [SEQ, D], BF16)
    dr["q_hbm"] = p.dram("q_hbm", [512, SEQ], BF16)
    dr["k_hbm"] = p.dram("k_hbm", [512, SEQ], BF16)
    dr["v_hbm"] = p.dram("v_hbm", [SEQ, 512], BF16)
    dr["qn_hbm"] = p.dram("qn_hbm", [4, SEQ], F32)
    dr["kn_hbm"] = p.dram("kn_hbm", [4, SEQ], F32)
    dr["attn_cc"] = p.dram("attn_cc", [512, SEQ], BF16)
    dr["attn_full"] = p.dram("attn_full", [D, SEQ], BF16)
    dr["h0_hbm"] = p.dram("h0_hbm", [T, I], BF16)
    dr["y_hbm"] = p.dram("y_hbm", [T, D], BF16)
    dr["h1_hbm"] = p.dram("h1_hbm", [T, 4 * D], BF16)
    dr["h2_hbm"] = p.dram("h2_hbm", [T, 4 * D], BF16)


def decl_weights(p, dr, which):
    if "l0ffn_in" in which:
        dr["pk_gu"] = p.dram("pk_gu", [2 * I, 256], I32, kind="ExternalInput")
        dr["scT_gu"] = p.dram("scT_gu", [2 * I, 16], F16, kind="ExternalInput")
        dr["pk_dn"] = p.dram("pk_dn", [D, 704], I32, kind="ExternalInput")
        dr["scT_dn"] = p.dram("scT_dn", [D, 44], F16, kind="ExternalInput")
    if "l1ffn_in" in which:
        dr["wfcT"] = p.dram("wfcT", [D, 4 * D], F32, kind="ExternalInput")
        dr["wprojT"] = p.dram("wprojT", [4 * D, D], F32, kind="ExternalInput")
    if "l0dq" in which:
        kind = "ExternalOutput" if "dqout" in which else (
            "ExternalInput" if "dqin" in which else "Internal")
        dr["wgu_dq"] = p.dram("wgu_dq", [2 * I, D], BF16, kind=kind)
        dr["wdn_dq"] = p.dram("wdn_dq", [D, I], BF16, kind=kind)
    if "l1t" in which:
        kind = "ExternalOutput" if "dqout" in which else (
            "ExternalInput" if "dqin" in which else "Internal")
        dr["tfc"] = p.dram("tfc", [D, 4 * D], BF16, kind=kind)
        dr["tproj"] = p.dram("tproj", [4 * D, D], BF16, kind=kind)
    if "l0attn" in which:
        dr["l0_wq"] = p.dram("l0_wq", [D, 1536], F32, kind="ExternalInput")
        dr["l0_wo"] = p.dram("l0_wo", [D, D], F32, kind="ExternalInput")
    if "l1attn" in which:
        dr["l1_wq"] = p.dram("l1_wq", [D, 1536], F32, kind="ExternalInput")
        dr["l1_bq"] = p.dram("l1_bq", [1536], F32, kind="ExternalInput")
        dr["l1_wo"] = p.dram("l1_wo", [D, D], F32, kind="ExternalInput")
        dr["l1_bo"] = p.dram("l1_bo", [D], F32, kind="ExternalInput")
        dr["slopes"] = p.dram("slopes", [4], F32, kind="ExternalInput")
    if "attn" in which or "l0attn" in which or "l1attn" in which:
        dr["tok0"] = p.dram("tok0", [1, 1], I32, kind="ExternalInput")


def build_fused():
    p = Prog()
    dr = {}
    dr["x_in"] = p.dram("x_in", [T, D], F32, kind="ExternalInput")
    dr["x_out"] = p.dram("x_out", [T, D], F32, kind="ExternalOutput")
    decl_weights(p, dr, ("l0ffn_in", "l1ffn_in", "l0dq", "l1t", "l0attn",
                         "l1attn"))
    decl_common_scratch(p, dr)
    R = alloc_common(p)
    s = std_sems(p)

    def body(em):
        emit_const_init(p, em, R, s)
        emit_load_x(p, em, R, dr["x_in"], s)
        emit_dequant(p, em, R, dr["pk_gu"], dr["scT_gu"], dr["wgu_dq"],
                     2 * I, 256, s)
        barrier(em, s["bar"])
        emit_dequant(p, em, R, dr["pk_dn"], dr["scT_dn"], dr["wdn_dq"],
                     D, 704, s)
        barrier(em, s["bar"])
        emit_ternarize(p, em, R, dr["wfcT"], dr["tfc"], D, 4 * D, s)
        barrier(em, s["bar"])
        emit_ternarize(p, em, R, dr["wprojT"], dr["tproj"], 4 * D, D, s)
        barrier(em, s["bar"])
        emit_ffn0_subl(p, em, R, s, dr)
        barrier(em, s["bar"])
        emit_attn_subl(p, em, R, s, dr, l0=True)
        barrier(em, s["bar"])
        emit_ffn0_subl(p, em, R, s, dr)
        barrier(em, s["bar"])
        emit_ffn1_subl(p, em, R, s, dr)
        barrier(em, s["bar"])
        emit_attn_subl(p, em, R, s, dr, l0=False)
        barrier(em, s["bar"])
        emit_ffn1_subl(p, em, R, s, dr)
        barrier(em, s["bar"])
        emit_store_x(p, em, R, dr["x_out"], s)
        em.wait("sync", s["out"])
        em.wait("gpsimd", s["out"])

    p.emit(body)
    return p.nc


def build_prep():
    p = Prog()
    dr = {}
    decl_weights(p, dr, ("l0ffn_in", "l1ffn_in", "l0dq", "l1t", "dqout"))
    R = alloc_common(p)
    s = std_sems(p)

    def body(em):
        emit_const_init(p, em, R, s)
        emit_dequant(p, em, R, dr["pk_gu"], dr["scT_gu"], dr["wgu_dq"],
                     2 * I, 256, s)
        barrier(em, s["bar"])
        emit_dequant(p, em, R, dr["pk_dn"], dr["scT_dn"], dr["wdn_dq"],
                     D, 704, s)
        barrier(em, s["bar"])
        emit_ternarize(p, em, R, dr["wfcT"], dr["tfc"], D, 4 * D, s)
        barrier(em, s["bar"])
        emit_ternarize(p, em, R, dr["wprojT"], dr["tproj"], 4 * D, D, s)
        em.wait("sync", s["st"])
        em.wait("gpsimd", s["st"])

    p.emit(body)
    return p.nc


def build_subl(which):
    p = Prog()
    dr = {}
    dr["x_in"] = p.dram("x_in", [T, D], F32, kind="ExternalInput")
    dr["x_out"] = p.dram("x_out", [T, D], F32, kind="ExternalOutput")
    decl_common_scratch(p, dr)
    if which == "ffn0":
        decl_weights(p, dr, ("l0dq", "dqin"))
    elif which == "ffn1":
        decl_weights(p, dr, ("l1t", "dqin"))
    elif which == "attn0":
        decl_weights(p, dr, ("l0attn",))
    elif which == "attn1":
        decl_weights(p, dr, ("l1attn",))
    R = alloc_common(p)
    s = std_sems(p)

    def body(em):
        emit_const_init(p, em, R, s)
        emit_load_x(p, em, R, dr["x_in"], s)
        if which == "ffn0":
            emit_ffn0_subl(p, em, R, s, dr)
        elif which == "ffn1":
            emit_ffn1_subl(p, em, R, s, dr)
        elif which == "attn0":
            emit_attn_subl(p, em, R, s, dr, l0=True)
        else:
            emit_attn_subl(p, em, R, s, dr, l0=False)
        barrier(em, s["bar"])
        emit_store_x(p, em, R, dr["x_out"], s)
        em.wait("sync", s["out"])
        em.wait("gpsimd", s["out"])

    p.emit(body)
    return p.nc


_CACHE = {}

_INPUTS = {
    "prep": ["pk_gu", "scT_gu", "pk_dn", "scT_dn", "wfcT", "wprojT"],
    "ffn0": ["x_in", "wgu_dq", "wdn_dq"],
    "ffn1": ["x_in", "tfc", "tproj"],
    "attn0": ["x_in", "l0_wq", "l0_wo", "tok0"],
    "attn1": ["x_in", "l1_wq", "l1_bq", "l1_wo", "l1_bo", "slopes", "tok0"],
    "fused": ["x_in", "pk_gu", "scT_gu", "pk_dn", "scT_dn", "wfcT", "wprojT",
              "l0_wq", "l0_wo", "l1_wq", "l1_bq", "l1_wo", "l1_bo", "slopes",
              "tok0"],
}


def _get(name, fn):
    if name not in _CACHE:
        _CACHE[name] = fn()
    return _CACHE[name]


def kernel(**inputs):
    per_core = shard_inputs(inputs)
    cores = list(range(8))
    if MODE == "fused":
        nc = _get("fused", build_fused)
        maps = [{k: pc[k] for k in _INPUTS["fused"]} for pc in per_core]
        res = run_bass_kernel_spmd(nc, maps, core_ids=cores)
        xs = [r["x_out"] for r in res.results]
    else:
        nc = _get("prep", build_prep)
        maps = [{k: pc[k] for k in _INPUTS["prep"]} for pc in per_core]
        res = run_bass_kernel_spmd(nc, maps, core_ids=cores)
        for c in range(8):
            for k in ("wgu_dq", "wdn_dq", "tfc", "tproj"):
                per_core[c][k] = res.results[c][k]
        seqs = ["ffn0", "attn0", "ffn0", "ffn1", "attn1", "ffn1"]
        xs = [pc["x_in"] for pc in per_core]
        for stage in seqs:
            nc = _get(stage, lambda w=stage: build_subl(w))
            for c in range(8):
                per_core[c]["x_in"] = xs[c]
            maps = [{k: pc[k] for k in _INPUTS[stage]} for pc in per_core]
            res = run_bass_kernel_spmd(nc, maps, core_ids=cores)
            xs = [r["x_out"] for r in res.results]
    out = np.zeros((2, SEQ, D), np.float32)
    for core in range(8):
        b, r = core // 4, core % 4
        out[b, r * 2048:(r + 1) * 2048] = xs[core]
    return out
